# version 01
# baseline (speedup 1.0000x reference)
import numpy as np
import ml_dtypes
from contextlib import ExitStack

import concourse.bass as bass
import concourse.mybir as mybir
from concourse.bass_utils import run_bass_kernel_spmd

F32 = mybir.dt.float32
BF16 = mybir.dt.bfloat16
AF = mybir.ActivationFunctionType
ALU = mybir.AluOpType
AX = mybir.AxisListType

D = 2048
DKV = 512
NH = 8
DH = 64
NQ = 4
SEQ = 2048
NSEQ = 2
QT = 512
NT = SEQ // QT
DC = D // 128
JC = DKV // 128
E = 8
F = 1024
FC = F // 128
EPS = 1e-5
SCALE = 0.125

ENGS = ("tensor", "vector", "scalar", "sync", "gpsimd")


class Builder:
    def __init__(self, n_seq=NSEQ, n_tiles=NT, dump=(), stop_after=None):
        self.n_seq = n_seq
        self.n_tiles = n_tiles
        self.stop_after = stop_after
        self.dump = set(dump)
        self.dumped = {}
        nc = bass.Bass("TRN2", target_bir_lowering=False, dynamic_dma_scratch_size=16384,
                       detect_race_conditions=False)
        self.nc = nc
        self.ops = {e: [] for e in ENGS}
        self.sems = {}
        self.cnt = {}
        self.lastw = {e: {} for e in ENGS}
        self.bank_free = [None] * 8

    def emit(self, eng, fn, waits=(), inc=None, n=1):
        w2 = []
        for w in waits:
            if w is None:
                continue
            s, v = w
            if v is None or v <= 0:
                continue
            if v > self.lastw[eng].get(s, 0):
                self.lastw[eng][s] = v
                w2.append((s, v))
        val = None
        if inc is not None:
            self.cnt[inc] += n
            val = self.cnt[inc]
        self.ops[eng].append((fn, w2, inc, n))
        return val

    def dma(self, eng, dst, src, waits=(), slow=False, sem=None):
        sname = sem or ("SX" if eng == "sync" else "SG")
        return self.emit(eng, lambda E, d=dst, s=src, sl=slow: E.dma_start(
            d, s, allow_slow_non_contiguous=sl), waits, inc=sname, n=16)

    def bw(self, i):
        return [self.bank_free[i]]

    def vgap(self):
        self.emit("vector", lambda v: v.memset(self.vjunk[:, :], 0.0))

    def vflush(self):
        return self.emit("vector", lambda v: v.drain(), inc="SV")

    def fence(self, eng):
        sname = "SX" if eng == "sync" else "SG"
        v = self.cnt[sname]
        self.emit(eng, lambda E, s=sname, v=v: E.wait_ge(self.sems[s], v))

    def dump_tensor(self, name, sbuf_ap, waits):
        if name not in self.dump or name in self.dumped:
            return
        parts, free = sbuf_ap.shape
        dt = sbuf_ap.dtype
        od = self.nc.dram_tensor(f"dbg_{name}", [parts, free], dt, kind="ExternalOutput").ap()
        self.dma("sync", od, sbuf_ap, waits=waits)
        self.fence("sync")
        self.dumped[name] = True

    def declare(self):
        nc = self.nc
        self.x_d = nc.dram_tensor("x", [NSEQ * SEQ, D], F32, kind="ExternalInput").ap()
        self.wqt_d = nc.dram_tensor("wqt", [DC * 128, D], F32, kind="ExternalInput").ap()
        self.wkt_d = nc.dram_tensor("wkt", [JC * 128, D], F32, kind="ExternalInput").ap()
        self.wvt_d = nc.dram_tensor("wvt", [JC * 128, D], F32, kind="ExternalInput").ap()
        self.mem_d = nc.dram_tensor("mem", [128, JC * DKV], F32, kind="ExternalInput").ap()
        self.mn_d = nc.dram_tensor("memnorm", [128, JC], F32, kind="ExternalInput").ap()
        self.mw_d = nc.dram_tensor("memw", [1, 1], F32, kind="ExternalInput").ap()
        self.rw_d = nc.dram_tensor("routw", [128, DC * E], F32, kind="ExternalInput").ap()
        self.gate_d = nc.dram_tensor("gate", [E * FC * 128, DC * 128], F32, kind="ExternalInput").ap()
        self.up_d = nc.dram_tensor("up", [E * FC * 128, DC * 128], F32, kind="ExternalInput").ap()
        self.down_d = nc.dram_tensor("down", [E * 8 * 128, FC * 256], F32, kind="ExternalInput").ap()
        self.idb_d = nc.dram_tensor("identbf", [128, 128], BF16, kind="ExternalInput").ap()
        self.idf_d = nc.dram_tensor("identf", [128, 128], F32, kind="ExternalInput").ap()
        self.ones_d = nc.dram_tensor("onesbf", [1, 128], BF16, kind="ExternalInput").ap()
        self.sel_d = nc.dram_tensor("selbf", [E, E * 128], BF16, kind="ExternalInput").ap()
        self.out_d = nc.dram_tensor("out", [NSEQ * SEQ, D], F32, kind="ExternalOutput").ap()


    def alloc(self, stack):
        nc = self.nc

        def sb(name, shape, dt):
            return stack.enter_context(nc.sbuf_tensor(name, shape, dt))

        self.xstage = [sb(f"xstage{i}", [128, D], F32) for i in range(2)]
        self.xnT = sb("xnT", [128, DC * QT], BF16)
        self.kT = sb("kT", [128, JC * SEQ], BF16)
        self.vx = sb("vx", [128, 16 * 8 * 65], BF16)
        self.wbuf = [sb(f"wbuf{i}", [128, DC * 128], BF16) for i in range(2)]
        self.qT = sb("qT", [128, 4 * QT], BF16)
        self.sigma = sb("sigma", [128, 4 * QT], BF16)
        self.elu0 = sb("elu0", [128, QT], BF16)
        self.elu1 = sb("elu1", [128, QT], BF16)
        self.vtmp = sb("vtmp", [128, QT], BF16)
        self.ptil = [sb(f"ptil{i}", [128, QT], BF16) for i in range(4)]
        self.gw = [sb(f"gw{i}", [128, DC * 128], BF16) for i in range(2)]
        self.uw = [sb(f"uw{i}", [128, DC * 128], BF16) for i in range(2)]
        self.dwb = [sb(f"dw{i}", [128, FC * 256], BF16) for i in range(2)]
        self.act = sb("act", [128, FC * QT], BF16)
        self.sgb = sb("sgb", [128, QT], BF16)
        self.ubb = sb("ubb", [128, QT], BF16)
        self.out_acc = sb("out_acc", [128, DC * QT], F32)
        self.outst = [sb("outst0", [128, D], F32)]
        self.xn_bf = sb("xn_bf", [128, D], BF16)
        self.mem_sb = sb("mem_sb", [128, JC * DKV], BF16)
        self.mn_sb = sb("mn_sb", [128, JC], BF16)
        self.rw_sb = sb("rw_sb", [128, DC * E], BF16)
        self.idb_sb = sb("idb_sb", [128, 128], BF16)
        self.idf_sb = sb("idf_sb", [128, 128], F32)
        self.ones_sb = sb("ones_sb", [1, 128], BF16)
        self.sel_sb = sb("sel_sb", [E, E * 128], BF16)
        self.s1 = sb("s1", [128, 1], F32)
        self.s2 = sb("s2", [128, 1], F32)
        self.s2b = sb("s2b", [128, 1], F32)
        self.mvec = sb("mvec", [128, 1], F32)
        self.varv = sb("varv", [128, 1], F32)
        self.msq = sb("msq", [128, 1], F32)
        self.rstd = sb("rstd", [128, 1], F32)
        self.epsb = sb("epsb", [128, 1], F32)
        self.vjunk = sb("vjunk", [128, 1], F32)
        self.nbv = sb("nbv", [128, 1], F32)
        self.Rf = sb("Rf", [32, QT], F32)
        self.Rt = sb("Rt", [128, 128], F32)
        self.cw32 = sb("cw32", [128, 128], F32)
        self.cwe = sb("cwe", [128, 128], F32)
        
        self.cw_bf = sb("cw_bf", [E, QT], BF16)
        self.rm1 = sb("rm1", [128, 8], F32)
        self.rm2 = sb("rm2", [128, 8], F32)
        self.rds = sb("rds", [128, 128], F32)
        self.rden = sb("rden", [128, 8], F32)
        self.rowf = sb("rowf", [1, QT], F32)
        self.rowb = [sb(f"rowb{i}", [1, QT], BF16) for i in range(2)]
        self.drow = sb("drow", [1, QT], F32)
        self.drowb = sb("drowb", [1, QT], BF16)
        self.bcS = sb("bcS", [128, QT], F32)
        self.dbcS = sb("dbcS", [128, QT], F32)
        self.cwbS = sb("cwbS", [128, QT], F32)
        self.amem_t = sb("amem_t", [128, QT], F32)
        self.wsc = sb("wsc", [1, 4], F32)
        self.lnw = sb("lnw", [1, 2], F32)
        self.pb = [stack.enter_context(nc.psum_tensor(f"pb{i}", [128, 512], F32))
                   for i in range(8)]
        for s in ("SX", "SG", "SP", "SA", "SV", "xst0", "xst1", "wb0", "wb1",
                  "gw0", "gw1", "uw0", "uw1", "dw0", "dw1", "ost"):
            self.sems[s] = stack.enter_context(nc.semaphore(s))
            self.cnt[s] = 0

    def prep(self):
        p = self
        p.emit("vector", lambda v: v.memset(p.epsb[:, :], EPS), inc="SV")
        p.emit("vector", lambda v: v.memset(p.Rf[:, :], 0.0), inc="SV")
        p.emit("vector", lambda v: v.memset(p.cw32[:, :], 0.0), inc="SV")
        p.dma("sync", p.idb_sb[:, :], p.idb_d)
        p.dma("sync", p.idf_sb[:, :], p.idf_d)
        p.dma("sync", p.ones_sb[:, :], p.ones_d)
        cx = p.dma("sync", p.sel_sb[:, :], p.sel_d)
        p.dma("gpsimd", p.mem_sb[:, :], p.mem_d)
        p.dma("gpsimd", p.mn_sb[:, :], p.mn_d)
        p.dma("gpsimd", p.rw_sb[:, :], p.rw_d)
        v_w = p.dma("gpsimd", p.wsc[0:1, 0:1], p.mw_d)
        self.consts_x = cx
        p.fence("sync")
        p.fence("gpsimd")
        a_w = p.emit("scalar", lambda s: s.activation(p.wsc[0:1, 1:2], p.wsc[0:1, 0:1], AF.Sigmoid),
                     waits=[("SG", v_w)], inc="SA")
        v_w2 = p.emit("vector", lambda v: v.tensor_scalar(p.wsc[0:1, 2:3], p.wsc[0:1, 1:2],
                                                          -1.0, 1.0, ALU.mult, ALU.add),
                      waits=[("SA", a_w)], inc="SV")
        vwf = p.vflush()
        a_l1 = p.emit("scalar", lambda s: s.activation(p.lnw[0:1, 0:1], p.wsc[0:1, 2:3], AF.Ln),
                      waits=[("SV", vwf)], inc="SA")
        self.lnw_ready = p.emit("scalar", lambda s: s.activation(p.lnw[0:1, 1:2], p.wsc[0:1, 1:2], AF.Ln),
                                waits=[("SA", a_l1)], inc="SA")
        self.consts_g = p.cnt["SG"]
        self.xn_bf_free = None
        self.wbuf_free = [None, None]
        self.xstage_free = [None, None]
        self.outst_free = [None]
        self.out_acc_free = None
        self.xnT_free = None
        self.kT_free = None
        self.vx_free = None
        self.act_free = None
        self.ln_sc_ev = {"s2r": None, "rstdr": None, "s2br": None}

    def ln_tile(self, seq, t0):
        p = self
        last = None
        for s4 in range(4):
            st = p.xstage[s4 % 2]
            row = seq * SEQ + t0 + s4 * 128
            xsem = f"xst{s4 % 2}"
            vx_ = p.dma("gpsimd", st[:, :], p.x_d[row:row + 128, :],
                        waits=[self.xstage_free[s4 % 2]], sem=xsem)
            vr = p.emit("vector", lambda v, st=st: v.reduce_sum(p.s1[:, :], st[:, :], axis=AX.X),
                        waits=[(xsem, vx_)], inc="SV")
            aq = p.emit("scalar", lambda s, st=st: s.activation(p.act[:, 0:D], st[:, :], AF.Square,
                                                                accum_out=p.s2[:, :]),
                        waits=[(xsem, vx_), self.act_free, self.ln_sc_ev["s2r"]], inc="SA")
            v1 = p.emit("vector", lambda v: v.tensor_scalar(p.mvec[:, :], p.s1[:, :],
                                                            1.0 / D, None, ALU.mult),
                        waits=[("SV", vr)], inc="SV")
            af = p.emit("scalar", lambda s: s.copy(p.s2b[:, :], p.s2[:, :]),
                        waits=[self.ln_sc_ev["s2br"]], inc="SA")
            v2 = p.emit("vector", lambda v: v.tensor_scalar(p.varv[:, :], p.s2b[:, :],
                                                            1.0 / D, None, ALU.mult),
                        waits=[("SA", af)], inc="SV")
            v3 = p.emit("vector", lambda v: v.tensor_mul(p.msq[:, :], p.mvec[:, :], p.mvec[:, :]),
                        inc="SV")
            p.vgap()
            v4 = p.emit("vector", lambda v: v.tensor_sub(p.varv[:, :], p.varv[:, :], p.msq[:, :]),
                        inc="SV")
            p.vgap()
            v4f = p.vflush()
            a1 = p.emit("scalar", lambda s: s.activation(p.rstd[:, :], p.varv[:, :], AF.Ln,
                                                         bias=p.epsb[:, :]),
                        waits=[("SV", v4f), self.ln_sc_ev["rstdr"]], inc="SA")
            a2 = p.emit("scalar", lambda s: s.activation(p.rstd[:, :], p.rstd[:, :], AF.Exp, scale=-0.5),
                        inc="SA")
            nb = p.emit("vector", lambda v: v.scalar_tensor_tensor(p.nbv[:, :], p.mvec[:, :],
                                                                   -1.0, p.rstd[:, :],
                                                                   ALU.mult, ALU.mult),
                        waits=[("SA", a2)], inc="SV")
            p.vgap()
            nbf = p.vflush()
            vn = p.emit("scalar", lambda s, st=st: s.activation(p.xn_bf[:, :], st[:, :], AF.Identity,
                                                                bias=p.nbv[:, :],
                                                                scale=p.rstd[:, :]),
                        waits=[("SV", nbf), self.xn_bf_free], inc="SA")
            tg = f"{getattr(self, 'ln_tag', '')}_s{s4}"
            self.dump_tensor(f"xst{tg}", st[:, :], [(xsem, vx_), ("SA", vn)])
            self.dump_tensor(f"s1{tg}", p.s1[:, :], [("SA", vn)])
            self.dump_tensor(f"s2{tg}", p.s2[:, :], [("SA", vn)])
            self.dump_tensor(f"s2b{tg}", p.s2b[:, :], [("SA", vn)])
            self.dump_tensor(f"mv{tg}", p.mvec[:, :], [("SA", vn)])
            self.dump_tensor(f"vv{tg}", p.varv[:, :], [("SA", vn)])
            self.dump_tensor(f"rs{tg}", p.rstd[:, :], [("SA", vn)])
            self.dump_tensor(f"nb{tg}", p.nbv[:, :], [("SA", vn)])
            self.xstage_free[s4 % 2] = ("SA", vn)
            self.ln_sc_ev["s2r"] = ("SA", af)
            self.ln_sc_ev["s2br"] = ("SV", v2)
            self.ln_sc_ev["rstdr"] = ("SA", vn)
            self.dump_tensor(f"xnb{getattr(self, 'ln_tag', '')}_s{s4}", p.xn_bf[:, :], [("SA", vn)])
            for dc in range(DC):
                bk = 6 + (dc % 2)
                vtp = p.emit("tensor",
                             lambda t, dc=dc, bk=bk: t.transpose(
                                 p.pb[bk][:, 0:64].bitcast(BF16),
                                 p.xn_bf[:, dc * 128:(dc + 1) * 128], p.idb_sb[:, :]),
                             waits=[("SA", vn)] + p.bw(bk), inc="SP")
                vev = p.emit("scalar",
                             lambda s, dc=dc, s4=s4, bk=bk: s.copy(
                                 p.xnT[:, dc * QT + s4 * 128: dc * QT + s4 * 128 + 128],
                                 p.pb[bk][:, 0:64].bitcast(BF16)),
                             waits=[("SP", vtp), self.xnT_free], inc="SA")
                p.bank_free[bk] = ("SA", vev)
                last = vev
            self.xn_bf_free = ("SA", last)
        return ("SA", last)

    def pass_a_ktile(self, seq, kt, tev):
        p = self
        for mat, wt_d, which in (("k", p.wkt_d, 0), ("v", p.wvt_d, 1)):
            for jc in range(JC):
                slot = (jc + which * JC) % 2
                wb = p.wbuf[slot]
                vg = p.dma("gpsimd", wb[:, :], wt_d[jc * 128:(jc + 1) * 128, :],
                           waits=[self.wbuf_free[slot]], sem=f"wb{slot}")
                bk = 0 if which == 0 else 2
                bk += jc % 2
                m = None
                for dc in range(DC):
                    m = p.emit("tensor",
                               lambda t, wb=wb, dc=dc, bk=bk, first=(dc == 0), lastc=(dc == DC - 1):
                               t.matmul(p.pb[bk][:, :], wb[:, dc * 128:(dc + 1) * 128],
                                        p.xnT[:, dc * QT:dc * QT + QT],
                                        start=first, stop=lastc),
                               waits=([(f"wb{slot}", vg), tev] + p.bw(bk)) if dc == 0 else [],
                               inc="SP" if dc == DC - 1 else None)
                self.wbuf_free[slot] = ("SP", m)
                if which == 0:
                    col = jc * SEQ + kt * QT
                    ae = p.emit("scalar",
                                lambda s, col=col, bk=bk: s.copy(
                                    p.kT[:, col:col + QT], p.pb[bk][:, :]),
                                waits=[("SP", m), self.kT_free], inc="SA")
                    p.bank_free[bk] = ("SA", ae)
                else:
                    ae = p.emit("scalar",
                                lambda s, bk=bk: s.copy(p.vtmp[:, :], p.pb[bk][:, :]),
                                waits=[("SP", m)], inc="SA")
                    p.bank_free[bk] = ("SA", ae)
                    for s4 in range(4):
                        tc = kt * 4 + s4
                        tb = 4 + (s4 % 2)
                        vtp = p.emit("tensor",
                                     lambda t, s4=s4, tb=tb: t.transpose(
                                         p.pb[tb][:, 0:64].bitcast(BF16),
                                         p.vtmp[:, s4 * 128:(s4 + 1) * 128], p.idb_sb[:, :]),
                                     waits=[("SA", ae)] + p.bw(tb), inc="SP")
                        off = tc * 520 + 2 * jc * 65
                        vev = p.emit("scalar",
                                     lambda s, off=off, tb=tb: s.copy(
                                         p.vx[:, off:off + 130]
                                         .rearrange("p (h d) -> p h d", d=65)[:, :, 0:64],
                                         p.pb[tb][:, 0:64].bitcast(BF16)
                                         .rearrange("p (h d) -> p h d", d=64)),
                                     waits=[("SP", vtp), self.vx_free], inc="SA")
                        p.bank_free[tb] = ("SA", vev)
                        self.pass_a_last = vev

    def router(self, tev):
        p = self
        bk = 6
        m = None
        for dc in range(DC):
            m = p.emit("tensor",
                       lambda t, dc=dc: t.matmul(p.pb[bk][0:8, :],
                                                 p.rw_sb[:, dc * E:(dc + 1) * E],
                                                 p.xnT[:, dc * QT:dc * QT + QT],
                                                 start=(dc == 0), stop=(dc == DC - 1)),
                       waits=([tev, ("SG", self.consts_g)] + p.bw(bk)) if dc == 0 else [],
                       inc="SP" if dc == DC - 1 else None)
        ve = p.emit("vector", lambda v: v.tensor_copy(p.Rf[0:8, :], p.pb[bk][0:8, :]),
                    waits=[("SP", m), self.rf_free], inc="SV")
        p.bank_free[bk] = ("SV", ve)
        vef = p.vflush()
        vtp = None
        for b in range(4):
            vtp = p.emit("tensor", lambda t, b=b: t.transpose(
                p.pb[7][:, b * 8:(b + 1) * 8], p.Rf[0:8, b * 128:(b + 1) * 128],
                p.idf_sb[0:8, 0:8]),
                waits=([("SV", vef)] if b == 0 else []) + (p.bw(7) if b == 0 else []),
                inc="SP" if b == 3 else None)
        vrt = p.emit("vector", lambda v: v.tensor_copy(p.Rt[0:128, 0:32], p.pb[7][:, 0:32]),
                     waits=[("SP", vtp), self.rt_free], inc="SV")
        p.bank_free[7] = ("SV", vrt)
        p.vgap()
        self.dump_tensor("Rf", p.Rf[:, :], [("SV", ve)])
        self.dump_tensor("Rt", p.Rt[:, :], [("SV", vrt)])
        R3 = p.Rt[0:128, 0:32].rearrange("p (b e) -> p b e", e=E)
        MSK1 = p.rds[:, 0:32].rearrange("p (b e) -> p b e", e=E)
        S3p = p.rds[:, 32:64].rearrange("p (b e) -> p b e", e=E)
        MSK2 = p.rds[:, 64:96].rearrange("p (b e) -> p b e", e=E)
        MSKD = p.rds[:, 96:128].rearrange("p (b e) -> p b e", e=E)
        vm1 = p.emit("vector", lambda v, R3=R3: v.reduce_max(
            p.rm1[:, 0:4].rearrange("p (b o) -> p b o", o=1), R3, axis=AX.X), inc="SV")
        m1b = p.rm1[:, 0:4].rearrange("p (b o) -> p b o", o=1).broadcast_to([128, 4, E])
        p.vgap()
        C3 = p.cw32[0:128, 0:32].rearrange("p (b e) -> p b e", e=E)
        vsub = p.emit("vector", lambda v, R3=R3, C3=C3, m1b=m1b: v.tensor_tensor(
            C3, R3, m1b, ALU.subtract), waits=[self.cw32_free], inc="SV")
        p.vgap()
        vms = p.emit("vector", lambda v, C3=C3, MSK1=MSK1: v.tensor_scalar(
            MSK1, C3, 0.0, None, ALU.is_ge), inc="SV")
        p.vgap()
        vsp = p.emit("vector", lambda v, R3=R3, MSK1=MSK1, S3p=S3p: v.scalar_tensor_tensor(
            S3p, MSK1, -1e30, R3, ALU.mult, ALU.add), inc="SV")
        p.vgap()
        vm2 = p.emit("vector", lambda v, S3p=S3p: v.reduce_max(
            p.rm2[:, 0:4].rearrange("p (b o) -> p b o", o=1), S3p, axis=AX.X), inc="SV")
        p.vgap()
        vd = p.emit("vector", lambda v: v.tensor_sub(p.rm2[:, 4:8], p.rm2[:, 0:4],
                                                     p.rm1[:, 0:4]), inc="SV")
        p.vgap()
        vdf = p.vflush()
        ae2 = p.emit("scalar", lambda s: s.activation(p.rden[:, 0:4], p.rm2[:, 4:8], AF.Exp),
                     waits=[("SV", vdf)], inc="SA")
        vd1 = p.emit("vector", lambda v: v.tensor_scalar(p.rden[:, 4:8], p.rden[:, 0:4],
                                                         1.0, None, ALU.add),
                     waits=[("SA", ae2)], inc="SV")
        p.vgap()
        vd2 = p.emit("vector", lambda v: v.reciprocal(p.rden[:, 0:4], p.rden[:, 4:8]), inc="SV")
        p.vgap()
        self.dump_tensor("rm1", p.rm1[:, :], [("SV", vm1)])
        self.dump_tensor("rm2", p.rm2[:, :], [("SV", vd)])
        self.dump_tensor("rdsA", p.rds[:, :], [("SV", vm2)])
        self.dump_tensor("rden", p.rden[:, :], [("SV", vd2)])
        m2b = p.rm2[:, 0:4].rearrange("p (b o) -> p b o", o=1).broadcast_to([128, 4, E])
        vd2b = p.emit("vector", lambda v, R3=R3, m2b=m2b, S3p=S3p: v.tensor_tensor(
            S3p, R3, m2b, ALU.subtract), inc="SV")
        p.vgap()
        vmk = p.emit("vector", lambda v, S3p=S3p, MSK2=MSK2: v.tensor_scalar(
            MSK2, S3p, 0.0, None, ALU.is_ge), inc="SV")
        p.vgap()
        denb = p.rden[:, 0:4].rearrange("p (b o) -> p b o", o=1).broadcast_to([128, 4, E])
        vmk2 = p.emit("vector", lambda v, MSK2=MSK2, denb=denb, MSKD=MSKD: v.tensor_tensor(
            MSKD, MSK2, denb, ALU.mult), inc="SV")
        p.vgap()
        E3 = p.cwe[0:128, 0:32].rearrange("p (b e) -> p b e", e=E)
        vsf = p.vflush()
        ae1 = p.emit("scalar", lambda s: s.activation(p.cwe[0:128, 0:32],
                                                      p.cw32[0:128, 0:32], AF.Exp),
                     waits=[("SV", vsf), self.cwe_free], inc="SA")
        vc1 = p.emit("vector", lambda v, C3=C3, E3=E3, MSKD=MSKD: v.tensor_tensor(
            C3, E3, MSKD, ALU.mult), waits=[("SA", ae1), ("SV", vmk2)], inc="SV")
        self.dump_tensor("rdsB", p.rds[:, :], [("SV", vmk2)])
        self.dump_tensor("cw32", p.cw32[:, :], [("SV", vc1)])
        vc2f = p.vflush()
        vcb = None
        for b in range(4):
            tb = 6 + (b % 2)
            mtb = p.emit("tensor", lambda t, b=b, tb=tb: t.transpose(
                p.pb[tb][0:8, 0:128], p.cw32[0:128, b * 8:(b + 1) * 8],
                p.idf_sb[:, :]),
                waits=[("SV", vc2f)] + p.bw(tb), inc="SP")
            vcb = p.emit("scalar", lambda s, b=b, tb=tb: s.copy(
                p.cw_bf[:, b * 128:(b + 1) * 128], p.pb[tb][0:E, 0:128]),
                waits=[("SP", mtb)], inc="SA")
            p.bank_free[tb] = ("SA", vcb)
        self.cw32_free = ("SV", vc2f)
        self.cwe_free = ("SV", vc1)
        self.rf_free = ("SV", vef)
        self.rt_free = ("SV", p.cnt["SV"])
        self.dump_tensor("cw", p.cw_bf[:, :], [("SA", vcb)])
        return ("SA", vcb)

    def qgroup(self, g, seq, tev):
        p = self
        ic_evs = []
        for icl in range(4):
            ic = g * 4 + icl
            slot = ic % 2
            wb = p.wbuf[slot]
            vg = p.dma("gpsimd", wb[:, :], p.wqt_d[ic * 128:(ic + 1) * 128, :],
                       waits=[self.wbuf_free[slot]], sem=f"wb{slot}")
            bk = 6 + (icl % 2)
            m = None
            for dc in range(DC):
                m = p.emit("tensor",
                           lambda t, wb=wb, dc=dc, bk=bk: t.matmul(
                               p.pb[bk][:, :], wb[:, dc * 128:(dc + 1) * 128],
                               p.xnT[:, dc * QT:dc * QT + QT],
                               start=(dc == 0), stop=(dc == DC - 1)),
                           waits=([(f"wb{slot}", vg), tev] + p.bw(bk)) if dc == 0 else [],
                           inc="SP" if dc == DC - 1 else None)
            self.wbuf_free[slot] = ("SP", m)
            ae = p.emit("scalar", lambda s, icl=icl, bk=bk: s.copy(
                p.qT[:, icl * QT:(icl + 1) * QT], p.pb[bk][:, :]),
                waits=[("SP", m), self.qT_free], inc="SA")
            if self.stop_after == "qmm":
                p.bank_free[bk] = ("SA", ae)
                ic_evs.append(("SA", ae))
                continue
            ELU_STOP = self.stop_after if (self.stop_after or "").startswith("qe") else None
            ve1 = p.emit("vector", lambda v, bk=bk: v.tensor_scalar(
                p.elu0[:, :], p.pb[bk][:, :], 0.0, None, ALU.min),
                waits=[("SP", m), ("SA", ae), self.elu_free], inc="SV")
            if ELU_STOP == "qe1":
                self.elu_free = ("SV", ve1)
                p.bank_free[bk] = ("SV", ve1)
                ic_evs.append(("SA", ae))
                continue
            ae2 = p.emit("scalar", lambda s: s.activation(p.elu1[:, :], p.elu0[:, :], AF.Exp),
                         waits=[("SV", ve1)], inc="SA")
            if ELU_STOP == "qe2":
                self.elu_free = ("SA", ae2)
                p.bank_free[bk] = ("SV", ve1)
                ic_evs.append(("SA", ae))
                continue
            ve2 = p.emit("vector", lambda v, bk=bk: v.tensor_scalar(
                p.elu0[:, :], p.pb[bk][:, :], 0.0, -1.0, ALU.max, ALU.add),
                waits=[("SA", ae2)], inc="SV")
            if ELU_STOP == "qe3":
                self.elu_free = ("SV", ve2)
                p.bank_free[bk] = ("SV", ve2)
                ic_evs.append(("SA", ae))
                continue
            p.vgap()
            ve3 = p.emit("vector", lambda v, icl=icl: v.tensor_add(
                p.sigma[:, icl * QT:(icl + 1) * QT], p.elu0[:, :], p.elu1[:, :]),
                waits=[self.sigma_free], inc="SV")
            self.elu_free = ("SV", ve3)
            p.bank_free[bk] = ("SV", ve2)
            ic_evs.append(("SA", ae))
        self.elu_free = ("SV", p.vflush())
        q_ready = ic_evs
        self.dump_tensor("qT", p.qT[:, :], q_ready)
        self.dump_tensor("sigma", p.sigma[:, :], [self.elu_free])
        if self.stop_after in ("qproj", "qmm", "qe1", "qe2", "qe3"):
            self.attn_last = 1
            return

        npair = 1 if self.stop_after == "pair0" else 4
        for j in range(npair):
            h0 = 2 * j
            idx0 = g * 8 + h0
            fcol = (idx0 // 2)
            exp_ev = [None] * 32
            ad_m = [None, None]
            for tc in range(16):
                sb0 = (tc % 2) * 2
                sb1 = sb0 + 1
                m0 = p.emit("tensor",
                            lambda t, j=j, tc=tc, sb0=sb0: t.matmul(
                                p.pb[sb0][:, :],
                                p.kT[0:64, j * SEQ + tc * 128: j * SEQ + tc * 128 + 128],
                                p.qT[0:64, j * QT:(j + 1) * QT],
                                start=True, stop=True, tile_position=(0, 0)),
                            waits=[q_ready[j], self.kT_ready] + p.bw(sb0), inc=None)
                m1 = p.emit("tensor",
                            lambda t, j=j, tc=tc, sb1=sb1: t.matmul(
                                p.pb[sb1][:, :],
                                p.kT[64:128, j * SEQ + tc * 128: j * SEQ + tc * 128 + 128],
                                p.qT[64:128, j * QT:(j + 1) * QT],
                                start=True, stop=True, tile_position=(64, 0)),
                            waits=p.bw(sb1), inc="SP")
                for hh in range(2):
                    pt = p.ptil[(2 * tc + hh) % 4]
                    sbk = sb0 + hh
                    aev = p.emit("scalar",
                                 lambda s, pt=pt, sbk=sbk: s.activation(
                                     pt[:, :], p.pb[sbk][:, :], AF.Exp, scale=SCALE),
                                 waits=[("SP", m1)], inc="SA")
                    exp_ev[2 * tc + hh] = aev
                    p.bank_free[sbk] = ("SA", aev)
                for hh in range(2):
                    h = h0 + hh
                    pt = p.ptil[(2 * tc + hh) % 4]
                    ab = 4 + hh
                    off = tc * 520 + h * 65
                    ad_m[hh] = p.emit("tensor",
                                      lambda t, off=off, pt=pt, ab=ab, tc=tc: t.matmul(
                                          p.pb[ab][0:65, :], p.vx[:, off:off + 65], pt[:, :],
                                          start=(tc == 0), stop=(tc == 15)),
                                      waits=[("SA", exp_ev[2 * tc + hh]), self.vx_ready]
                                      + (p.bw(ab) if tc == 0 else []),
                                      inc="SP" if tc == 15 else None)
            for hh in range(2):
                ab = 4 + hh
                bb = 6 + hh
                al = p.emit("scalar", lambda s, ab=ab: s.activation(
                    p.rowf[0:1, :], p.pb[ab][64:65, :], AF.Ln),
                    waits=[("SP", ad_m[hh]), self.rowf_free], inc="SA")
                ae = p.emit("scalar", lambda s, hh=hh: s.activation(
                    p.rowb[hh][0:1, :], p.rowf[0:1, :], AF.Exp, scale=-1.0,
                    bias=p.lnw[0:1, 0:1]),
                    waits=[("SA", self.lnw_ready), self.rowb_free[hh]], inc="SA")
                self.rowf_free = ("SA", ae)
                mb = p.emit("tensor", lambda t, hh=hh, bb=bb: t.matmul(
                    p.pb[bb][0:64, :], p.ones_sb[0:1, 0:64], p.rowb[hh][0:1, :],
                    start=True, stop=True),
                    waits=[("SA", ae)] + p.bw(bb), inc="SP")
                be = p.emit("scalar", lambda s, hh=hh, bb=bb: s.copy(
                    p.bcS[hh * 64:hh * 64 + 64, :], p.pb[bb][0:64, :]),
                    waits=[("SP", mb), self.bcS_free], inc="SA")
                p.bank_free[bb] = ("SA", be)
                self.rowb_free[hh] = ("SP", mb)
                vo = p.emit("vector", lambda v, hh=hh, ab=ab, fcol=fcol: v.tensor_tensor(
                    p.out_acc[hh * 64:hh * 64 + 64, fcol * QT:(fcol + 1) * QT],
                    p.pb[ab][0:64, :], p.bcS[hh * 64:hh * 64 + 64, :], ALU.mult),
                    waits=[("SA", be), self.out_acc_free], inc="SV")
                p.bank_free[ab] = ("SV", vo)
                self.bcS_free = ("SV", vo)
                self.attn_last = vo

        if self.stop_after in ("pair0", "heads"):
            return
        dm = None
        for icl in range(4):
            dm = p.emit("tensor", lambda t, icl=icl: t.matmul(
                p.pb[4][0:1, :], p.mn_sb[:, icl:icl + 1],
                p.sigma[:, icl * QT:(icl + 1) * QT],
                start=(icl == 0), stop=(icl == 3)),
                waits=([self.elu_free, ("SG", self.consts_g)] + p.bw(4)) if icl == 0 else [],
                inc="SP" if icl == 3 else None)
        vdr = p.emit("vector", lambda v: v.reciprocal(p.drow[0:1, :], p.pb[4][0:1, :]),
                     waits=[("SP", dm), self.drow_free], inc="SV")
        vdb = p.emit("vector", lambda v: v.tensor_scalar(p.drowb[0:1, :], p.drow[0:1, :],
                                                         p.wsc[0:1, 1:2], None, ALU.mult),
                     waits=[self.drowb_free], inc="SV")
        p.bank_free[4] = ("SV", vdr)
        self.drow_free = ("SV", vdb)
        vdb = p.vflush()
        mdb = p.emit("tensor", lambda t: t.matmul(
            p.pb[5][:, :], p.ones_sb[0:1, :], p.drowb[0:1, :], start=True, stop=True),
            waits=[("SV", vdb)] + p.bw(5), inc="SP")
        bde = p.emit("scalar", lambda s: s.copy(p.dbcS[:, :], p.pb[5][:, :]),
                     waits=[("SP", mdb), self.dbcS_free], inc="SA")
        p.bank_free[5] = ("SA", bde)
        self.drowb_free = ("SP", mdb)
        vt_ = None
        for jcl in range(4):
            ab = jcl % 2
            mm = None
            for icl in range(4):
                mm = p.emit("tensor", lambda t, icl=icl, jcl=jcl, ab=ab: t.matmul(
                    p.pb[ab][:, :], p.mem_sb[:, icl * DKV + jcl * 128: icl * DKV + jcl * 128 + 128],
                    p.sigma[:, icl * QT:(icl + 1) * QT],
                    start=(icl == 0), stop=(icl == 3)),
                    waits=(p.bw(ab) + [self.elu_free, ("SG", self.consts_g)])
                    if icl == 0 else [], inc="SP" if icl == 3 else None)
            vt_ = p.emit("vector", lambda v, ab=ab: v.tensor_tensor(
                p.amem_t[:, :], p.pb[ab][:, :], p.dbcS[:, :], ALU.mult),
                waits=[("SP", mm), ("SA", bde), self.amem_free], inc="SV")
            p.vgap()
            fcol = g * 4 + jcl
            va = p.emit("vector", lambda v, fcol=fcol: v.tensor_add(
                p.out_acc[:, fcol * QT:(fcol + 1) * QT],
                p.out_acc[:, fcol * QT:(fcol + 1) * QT], p.amem_t[:, :]),
                inc="SV")
            p.bank_free[ab] = ("SV", vt_)
            self.amem_free = ("SV", va)
            self.attn_last = va
        self.dbcS_free = ("SV", vt_)
        self.qT_free = ("SP", p.cnt["SP"])
        self.sigma_free = ("SP", p.cnt["SP"])

    def ffn_expert(self, e, qt_idx, tev, cw_ev):
        p = self
        mcw = p.emit("tensor", lambda t, e=e: t.matmul(
            p.pb[6][:, :], p.sel_sb[:, e * 128:(e + 1) * 128], p.cw_bf[:, :],
            start=True, stop=True),
            waits=[cw_ev, ("SX", self.consts_x)] + p.bw(6), inc="SP")
        cbe = p.emit("scalar", lambda s: s.copy(p.cwbS[:, :], p.pb[6][:, :]),
                     waits=[("SP", mcw), self.cwbS_free], inc="SA")
        p.bank_free[6] = ("SA", cbe)

        act_evs = []
        for fc in range(FC):
            slot = fc % 2
            r0 = (e * FC + fc) * 128
            gsrc = self.gate_d[r0:r0 + 128, :]
            usrc = self.up_d[r0:r0 + 128, :]
            vgg = p.dma("gpsimd", p.gw[slot][:, :], gsrc, waits=[self.gw_free[slot]],
                        sem=f"gw{slot}")
            vgu = p.dma("gpsimd", p.uw[slot][:, :], usrc, waits=[self.uw_free[slot]],
                        sem=f"uw{slot}")
            gb = (fc % 2) * 2
            ub = gb + 1
            mg = None
            for dc in range(DC):
                mg = p.emit("tensor", lambda t, slot=slot, dc=dc, gb=gb: t.matmul(
                    p.pb[gb][:, :], p.gw[slot][:, dc * 128:(dc + 1) * 128],
                    p.xnT[:, dc * QT:dc * QT + QT],
                    start=(dc == 0), stop=(dc == DC - 1)),
                    waits=([(f"gw{slot}", vgg), tev] + p.bw(gb)) if dc == 0 else [],
                    inc="SP" if dc == DC - 1 else None)
            mu = None
            for dc in range(DC):
                mu = p.emit("tensor", lambda t, slot=slot, dc=dc, ub=ub: t.matmul(
                    p.pb[ub][:, :], p.uw[slot][:, dc * 128:(dc + 1) * 128],
                    p.xnT[:, dc * QT:dc * QT + QT],
                    start=(dc == 0), stop=(dc == DC - 1)),
                    waits=([(f"uw{slot}", vgu)] + p.bw(ub)) if dc == 0 else [],
                    inc="SP" if dc == DC - 1 else None)
            self.gw_free[slot] = ("SP", mg)
            self.uw_free[slot] = ("SP", mu)
            sil = p.emit("scalar", lambda s, gb=gb: s.activation(
                p.sgb[:, :], p.pb[gb][:, :], AF.Silu),
                waits=[("SP", mg), ("SA", cbe), self.sgb_free], inc="SA")
            p.bank_free[gb] = ("SA", sil)
            vu = p.emit("vector", lambda v, ub=ub: v.tensor_tensor(
                p.ubb[:, :], p.pb[ub][:, :], p.cwbS[:, :], ALU.mult),
                waits=[("SP", mu), ("SA", cbe), self.ubb_free], inc="SV")
            p.bank_free[ub] = ("SV", vu)
            va = p.emit("vector", lambda v, fc=fc: v.tensor_tensor(
                p.act[:, fc * QT:(fc + 1) * QT], p.sgb[:, :], p.ubb[:, :], ALU.mult),
                waits=[("SA", sil), self.act_free], inc="SV")
            self.sgb_free = ("SV", va)
            self.ubb_free = ("SV", va)
            act_evs.append(("SV", va))
        act_all = ("SV", p.vflush())

        for dq in range(8):
            slot = dq % 2
            r0 = (e * 8 + dq) * 128
            dsrc = self.down_d[r0:r0 + 128, :]
            vgd = p.dma("gpsimd", p.dwb[slot][:, :], dsrc, waits=[self.dw_free[slot]],
                        sem=f"dw{slot}")
            for dcl in range(2):
                dc = dq * 2 + dcl
                db = 4 + (dcl % 2)
                md = None
                for fc in range(FC):
                    md = p.emit("tensor", lambda t, slot=slot, fc=fc, dcl=dcl, db=db: t.matmul(
                        p.pb[db][:, :], p.dwb[slot][:, fc * 256 + dcl * 128: fc * 256 + dcl * 128 + 128],
                        p.act[:, fc * QT:(fc + 1) * QT],
                        start=(fc == 0), stop=(fc == FC - 1)),
                        waits=([(f"dw{slot}", vgd), act_all] + p.bw(db)) if fc == 0 else [],
                        inc="SP" if fc == FC - 1 else None)
                if dcl == 1:
                    self.dw_free[slot] = ("SP", md)
                vo = p.emit("vector", lambda v, dc=dc, db=db: v.tensor_add(
                    p.out_acc[:, dc * QT:(dc + 1) * QT],
                    p.out_acc[:, dc * QT:(dc + 1) * QT], p.pb[db][:, :]),
                    waits=[("SP", md)], inc="SV")
                p.bank_free[db] = ("SV", vo)
                self.ffn_last = vo
        self.ffn_flush = p.vflush()
        self.act_free = ("SP", p.cnt["SP"])
        self.cwbS_free = ("SV", self.ffn_last)

    def epilogue(self, seq, qt):
        p = self
        for t4 in range(4):
            st = p.xstage[t4 % 2]
            row = seq * SEQ + qt * QT + t4 * 128
            xsem = f"xst{t4 % 2}"
            vx_ = p.dma("gpsimd", st[:, :], p.x_d[row:row + 128, :],
                        waits=[self.xstage_free[t4 % 2]], sem=xsem)
            ob = p.outst[0]
            last_add = None
            for fc in range(DC):
                tb = 6 + (fc % 2)
                vtp = p.emit("tensor", lambda t, fc=fc, t4=t4, tb=tb: t.transpose(
                    p.pb[tb][:, 0:128], p.out_acc[:, fc * QT + t4 * 128: fc * QT + t4 * 128 + 128],
                    p.idf_sb[:, :]),
                    waits=([("SV", self.ffn_flush)] if fc == 0 else []) + p.bw(tb), inc="SP")
                va = p.emit("vector", lambda v, fc=fc, tb=tb, st=st, ob=ob: v.tensor_tensor(
                    ob[:, fc * 128:(fc + 1) * 128], st[:, fc * 128:(fc + 1) * 128],
                    p.pb[tb][:, 0:128], ALU.add),
                    waits=[("SP", vtp), (xsem, vx_), self.outst_free[0]], inc="SV")
                p.bank_free[tb] = ("SV", va)
                last_add = va
            laf = p.vflush()
            sx = p.dma("sync", p.out_d[row:row + 128, :], ob[:, :],
                       waits=[("SV", laf)], sem="ost")
            self.outst_free[0] = ("ost", sx)
            self.xstage_free[t4 % 2] = ("SV", last_add)
        self.out_acc_free = ("SP", p.cnt["SP"])

    def build(self):
        p = self
        self.declare()
        with ExitStack() as stack:
            self.alloc(stack)
            self.prep()
            self.elu_free = None
            self.qT_free = None
            self.sigma_free = None
            self.rowf_free = None
            self.rowb_free = [None, None]
            self.bcS_free = None
            self.drow_free = None
            self.drowb_free = None
            self.dbcS_free = None
            self.amem_free = None
            self.cwbS_free = None
            self.sgb_free = None
            self.ubb_free = None
            self.gw_free = [None, None]
            self.uw_free = [None, None]
            self.dw_free = [None, None]
            self.rf_free = None
            self.cw32_free = None
            self.rt_free = None
            self.cwe_free = None
            self.pass_a_sem = "SV"
            self.ffn_last = None
            self.attn_last = None
            self.consts_g = self.cnt["SG"]
            SA_ = self.stop_after
            for seq in range(self.n_seq):
                if SA_ == "prep":
                    break
                vo = p.emit("vector", lambda v: v.memset(
                    p.vx[:, :].rearrange("p (t h d) -> p t h d", h=8, d=65)[:, :, :, 64:65], 1.0),
                    waits=[self.vx_free], inc="SV")
                self.vx_free = ("SV", vo)
                for kt in range(NT):
                    if SA_ == "passa0" and kt > 0:
                        break
                    self.ln_tag = f"A{kt}"
                    tev = self.ln_tile(seq, kt * QT)
                    if SA_ in ("ln0", "ln0s"):
                        break
                    self.pass_a_ktile(seq, kt, tev)
                    self.xnT_free = ("SP", self.cnt["SP"])
                self.kT_ready = ("SA", self.cnt["SA"])
                self.vx_ready = ("SA", self.pass_a_last)
                self.dump_tensor("kT", p.kT[:, :], [self.kT_ready])
                self.dump_tensor("vx", p.vx[:, :], [self.vx_ready])
                if SA_ in ("prep", "ln0", "passa0", "passa"):
                    break
                for qt in range(self.n_tiles):
                    self.ln_tag = f"B{qt}"
                    tev = self.ln_tile(seq, qt * QT)
                    self.dump_tensor("xnT", p.xnT[:, :], [tev])
                    if SA_ == "lnq":
                        break
                    cw_ev = self.router(tev)
                    if SA_ == "router":
                        break
                    if SA_ in ("qmm", "qproj", "pair0", "heads", "qe1", "qe2", "qe3"):
                        self.qgroup(0, seq, tev)
                        break
                    ngq = NQ if SA_ is None or not SA_.startswith("qg") else int(SA_[2:]) + 1
                    for g in range(ngq):
                        self.qgroup(g, seq, tev)
                    if SA_ is not None and (SA_.startswith("qg") or SA_ in ("qmm", "qproj", "pair0", "heads", "qe1", "qe2", "qe3")):
                        break
                    self.dump_tensor("outacc_attn", p.out_acc[:, :], [("SV", self.attn_last)])
                    self.dump_tensor("cw2", p.cw_bf[:, :], [("SV", self.attn_last)])
                    nfe = E if SA_ is None or not SA_.startswith("ffn") else int(SA_[3:]) + 1
                    for e in range(nfe):
                        self.ffn_expert(e, qt, tev, cw_ev)
                    if SA_ is not None and SA_.startswith("ffn"):
                        break
                    self.dump_tensor("outacc", p.out_acc[:, :], [("SV", self.ffn_last)])
                    self.epilogue(seq, qt)
                    self.xnT_free = ("SP", self.cnt["SP"])
                if SA_ is not None:
                    break
                self.kT_free = ("SP", self.cnt["SP"])
                self.vx_free = ("SP", self.cnt["SP"])

            final_o = self.cnt["ost"]
            final_x = self.cnt["SX"]
            p.emit("sync", lambda s: s.wait_ge(self.sems["ost"], final_o),
                   waits=[("SX", final_x)])

            block = stack.enter_context(self.nc.Block())

            def runner(eng_name):
                def _run(eng):
                    for fn, waits, inc, n in self.ops[eng_name]:
                        for s, v in waits:
                            eng.wait_ge(self.sems[s], v)
                        inst = fn(eng)
                        if inc is not None:
                            inst.then_inc(self.sems[inc], n)
                return _run

            block.tensor(runner("tensor"))
            block.vector(runner("vector"))
            block.scalar(runner("scalar"))
            block.sync(runner("sync"))
            block.gpsimd(runner("gpsimd"))
        return self.nc

    vx_free = None
    kT_free = None
    kT_ready = None
    vx_ready = None
    pass_a_last = None


def build_program(n_seq=NSEQ, n_tiles=NT, dump=(), stop_after=None):
    b = Builder(n_seq=n_seq, n_tiles=n_tiles, dump=dump, stop_after=stop_after)
    nc = b.build()
    return nc, b


_CONSTS = None


def _consts():
    global _CONSTS
    if _CONSTS is None:
        identbf = np.eye(128, dtype=ml_dtypes.bfloat16)
        identf = np.eye(128, dtype=np.float32)
        onesbf = np.ones((1, 128), dtype=ml_dtypes.bfloat16)
        selbf = np.zeros((E, E * 128), dtype=ml_dtypes.bfloat16)
        for e in range(E):
            selbf[e, e * 128:(e + 1) * 128] = 1.0
        _CONSTS = dict(identbf=identbf, identf=identf, onesbf=onesbf, selbf=selbf)
    return _CONSTS


def _relayout(wq, wk, wv, mem, mn, rw, gate, up, down):
    wqt = np.ascontiguousarray(wq.reshape(16, 128, 16, 128).transpose(0, 3, 2, 1)
                               .reshape(DC * 128, D))
    wkt = np.ascontiguousarray(wk.reshape(4, 128, 16, 128).transpose(0, 3, 2, 1)
                               .reshape(JC * 128, D))
    wvt = np.ascontiguousarray(wv.reshape(4, 128, 16, 128).transpose(0, 3, 2, 1)
                               .reshape(JC * 128, D))
    mem_r = np.ascontiguousarray(mem.reshape(4, 128, DKV).transpose(1, 0, 2)
                                 .reshape(128, JC * DKV))
    mn_r = np.ascontiguousarray(mn.reshape(4, 128).T)
    rw_r = np.ascontiguousarray(rw.reshape(16, 128, E).transpose(1, 0, 2)
                                .reshape(128, DC * E))
    gate_r = np.ascontiguousarray(gate.reshape(E, 16, 128, 8, 128)
                                  .transpose(0, 3, 2, 1, 4).reshape(E * FC * 128, 2048))
    up_r = np.ascontiguousarray(up.reshape(E, 16, 128, 8, 128)
                                .transpose(0, 3, 2, 1, 4).reshape(E * FC * 128, 2048))
    down_r = np.ascontiguousarray(down.reshape(E, 8, 128, 8, 256)
                                  .transpose(0, 3, 2, 1, 4).reshape(E * 8 * 128, 2048))
    return wqt, wkt, wvt, mem_r, mn_r, rw_r, gate_r, up_r, down_r


def kernel(L0_attn_x, L0_attn_Wq, L0_attn_Wk, L0_attn_Wv, L0_attn_memory,
           L0_attn_memory_norm, L0_attn_memory_weight, L0_ffn_router_w,
           L0_ffn_gate_proj, L0_ffn_up_proj, L0_ffn_down_proj):
    x = np.ascontiguousarray(np.asarray(L0_attn_x, dtype=np.float32))
    wq = np.asarray(L0_attn_Wq, dtype=np.float32)
    wk = np.asarray(L0_attn_Wk, dtype=np.float32)
    wv = np.asarray(L0_attn_Wv, dtype=np.float32)
    mem = np.asarray(L0_attn_memory, dtype=np.float32)
    mn = np.asarray(L0_attn_memory_norm, dtype=np.float32)
    mw = np.asarray(L0_attn_memory_weight, dtype=np.float32).reshape(1, 1)
    rw = np.asarray(L0_ffn_router_w, dtype=np.float32)
    gate = np.asarray(L0_ffn_gate_proj, dtype=np.float32)
    up = np.asarray(L0_ffn_up_proj, dtype=np.float32)
    down = np.asarray(L0_ffn_down_proj, dtype=np.float32)
    wqt, wkt, wvt, mem_r, mn_r, rw_r, gate_r, up_r, down_r = _relayout(
        wq, wk, wv, mem, mn, rw, gate, up, down)
    c = _consts()

    nc, _ = build_program()
    in_maps = []
    for core in range(8):
        xs = x[2 * core:2 * core + 2].reshape(NSEQ * SEQ, D)
        in_maps.append(dict(x=np.ascontiguousarray(xs), wqt=wqt, wkt=wkt, wvt=wvt,
                            mem=mem_r, memnorm=mn_r, memw=mw, routw=rw_r,
                            gate=gate_r, up=up_r, down=down_r,
                            identbf=c["identbf"], identf=c["identf"],
                            onesbf=c["onesbf"], selbf=c["selbf"]))
    res = run_bass_kernel_spmd(nc, in_maps, core_ids=list(range(8)))
    outs = [res.results[i]["out"].reshape(NSEQ, SEQ, D) for i in range(8)]
    return np.concatenate(outs, axis=0)


# version 02 (final)
# speedup vs baseline: 1.0192x; 1.0192x over previous
import numpy as np
import ml_dtypes
from contextlib import ExitStack

import concourse.bass as bass
import concourse.mybir as mybir
from concourse.bass_utils import run_bass_kernel_spmd

F32 = mybir.dt.float32
BF16 = mybir.dt.bfloat16
AF = mybir.ActivationFunctionType
ALU = mybir.AluOpType
AX = mybir.AxisListType

D = 2048
DKV = 512
NH = 8
DH = 64
NQ = 4
SEQ = 2048
NSEQ = 2
QT = 512
NT = SEQ // QT
DC = D // 128
JC = DKV // 128
E = 8
F = 1024
FC = F // 128
EPS = 1e-5
SCALE = 0.125

ENGS = ("tensor", "vector", "scalar", "sync", "gpsimd")


class Builder:
    def __init__(self, n_seq=NSEQ, n_tiles=NT, dump=(), stop_after=None):
        self.n_seq = n_seq
        self.n_tiles = n_tiles
        self.stop_after = stop_after
        self.dump = set(dump)
        self.dumped = {}
        nc = bass.Bass("TRN2", target_bir_lowering=False, dynamic_dma_scratch_size=16384,
                       detect_race_conditions=False)
        self.nc = nc
        self.ops = {e: [] for e in ENGS}
        self.sems = {}
        self.cnt = {}
        self.lastw = {e: {} for e in ENGS}
        self.bank_free = [None] * 8

    def emit(self, eng, fn, waits=(), inc=None, n=1):
        w2 = []
        flat = []
        for w in waits:
            if isinstance(w, list):
                flat.extend(w)
            else:
                flat.append(w)
        for w in flat:
            if w is None:
                continue
            s, v = w
            if v is None or v <= 0:
                continue
            if v > self.lastw[eng].get(s, 0):
                self.lastw[eng][s] = v
                w2.append((s, v))
        val = None
        if inc is not None:
            self.cnt[inc] += n
            val = self.cnt[inc]
        self.ops[eng].append((fn, w2, inc, n))
        return val

    def dma(self, eng, dst, src, waits=(), slow=False, sem=None):
        sname = sem or ("SX" if eng == "sync" else "SG")
        return self.emit(eng, lambda E, d=dst, s=src, sl=slow: E.dma_start(
            d, s, allow_slow_non_contiguous=sl), waits, inc=sname, n=16)

    def bw(self, i):
        return [self.bank_free[i]]

    def vgap(self):
        self.emit("vector", lambda v: v.memset(self.vjunk[:, :], 0.0))

    def vflush(self):
        return self.emit("vector", lambda v: v.drain(), inc="SV")

    def fence(self, eng):
        sname = "SX" if eng == "sync" else "SG"
        v = self.cnt[sname]
        self.emit(eng, lambda E, s=sname, v=v: E.wait_ge(self.sems[s], v))

    def dump_tensor(self, name, sbuf_ap, waits):
        if name not in self.dump or name in self.dumped:
            return
        parts, free = sbuf_ap.shape
        dt = sbuf_ap.dtype
        od = self.nc.dram_tensor(f"dbg_{name}", [parts, free], dt, kind="ExternalOutput").ap()
        self.dma("sync", od, sbuf_ap, waits=waits)
        self.fence("sync")
        self.dumped[name] = True

    def declare(self):
        nc = self.nc
        self.x_d = nc.dram_tensor("x", [NSEQ * SEQ, D], F32, kind="ExternalInput").ap()
        self.wqt_d = nc.dram_tensor("wqt", [DC * 128, D], F32, kind="ExternalInput").ap()
        self.wkt_d = nc.dram_tensor("wkt", [JC * 128, D], F32, kind="ExternalInput").ap()
        self.wvt_d = nc.dram_tensor("wvt", [JC * 128, D], F32, kind="ExternalInput").ap()
        self.mem_d = nc.dram_tensor("mem", [128, JC * DKV], F32, kind="ExternalInput").ap()
        self.mn_d = nc.dram_tensor("memnorm", [128, JC], F32, kind="ExternalInput").ap()
        self.mw_d = nc.dram_tensor("memw", [1, 1], F32, kind="ExternalInput").ap()
        self.rw_d = nc.dram_tensor("routw", [128, DC * E], F32, kind="ExternalInput").ap()
        self.gate_d = nc.dram_tensor("gate", [E * FC * 128, DC * 128], F32, kind="ExternalInput").ap()
        self.up_d = nc.dram_tensor("up", [E * FC * 128, DC * 128], F32, kind="ExternalInput").ap()
        self.down_d = nc.dram_tensor("down", [E * 8 * 128, FC * 256], F32, kind="ExternalInput").ap()
        self.idb_d = nc.dram_tensor("identbf", [128, 128], BF16, kind="ExternalInput").ap()
        self.idf_d = nc.dram_tensor("identf", [128, 128], F32, kind="ExternalInput").ap()
        self.ones_d = nc.dram_tensor("onesbf", [1, 128], BF16, kind="ExternalInput").ap()
        self.sel_d = nc.dram_tensor("selbf", [E, E * 128], BF16, kind="ExternalInput").ap()
        self.out_d = nc.dram_tensor("out", [NSEQ * SEQ, D], F32, kind="ExternalOutput").ap()
        self.gsc_d = nc.dram_tensor("gsc", [E * FC * 128, DC * 128], BF16, kind="Internal").ap()
        self.usc_d = nc.dram_tensor("usc", [E * FC * 128, DC * 128], BF16, kind="Internal").ap()
        self.dsc_d = nc.dram_tensor("dsc", [E * 8 * 128, FC * 256], BF16, kind="Internal").ap()


    def alloc(self, stack):
        nc = self.nc

        def sb(name, shape, dt):
            return stack.enter_context(nc.sbuf_tensor(name, shape, dt))

        self.xstage = [sb(f"xstage{i}", [128, D], F32) for i in range(2)]
        self.xnT = sb("xnT", [128, DC * QT], BF16)
        self.kT = sb("kT", [128, JC * SEQ], BF16)
        self.vx = sb("vx", [128, 16 * 8 * 65], BF16)
        self.wbuf = [sb(f"wbuf{i}", [128, DC * 128], BF16) for i in range(2)]
        self.qT = sb("qT", [128, 4 * QT], BF16)
        self.sigma = sb("sigma", [128, 4 * QT], BF16)
        self.elu0 = sb("elu0", [128, QT], BF16)
        self.elu1 = sb("elu1", [128, QT], BF16)
        self.vtmp = sb("vtmp", [128, QT], BF16)
        self.ptil = [sb(f"ptil{i}", [128, QT], BF16) for i in range(4)]
        self.gw = [sb(f"gw{i}", [128, DC * 128], BF16) for i in range(2)]
        self.uw = [sb(f"uw{i}", [128, DC * 128], BF16) for i in range(2)]
        self.dwb = [sb(f"dw{i}", [128, FC * 256], BF16) for i in range(2)]
        self.act = sb("act", [128, FC * QT], BF16)
        self.sgb = sb("sgb", [128, QT], BF16)
        self.ubb = sb("ubb", [128, QT], BF16)
        self.out_acc = sb("out_acc", [128, DC * QT], F32)
        self.outst = [sb("outst0", [128, D], F32)]
        self.xn_bf = sb("xn_bf", [128, D], BF16)
        self.mem_sb = sb("mem_sb", [128, JC * DKV], BF16)
        self.mn_sb = sb("mn_sb", [128, JC], BF16)
        self.rw_sb = sb("rw_sb", [128, DC * E], BF16)
        self.idb_sb = sb("idb_sb", [128, 128], BF16)
        self.idf_sb = sb("idf_sb", [128, 128], F32)
        self.ones_sb = sb("ones_sb", [1, 128], BF16)
        self.sel_sb = sb("sel_sb", [E, E * 128], BF16)
        self.s1 = sb("s1", [128, 1], F32)
        self.s2 = sb("s2", [128, 1], F32)
        self.s2b = sb("s2b", [128, 1], F32)
        self.mvec = sb("mvec", [128, 1], F32)
        self.varv = sb("varv", [128, 1], F32)
        self.msq = sb("msq", [128, 1], F32)
        self.rstd = sb("rstd", [128, 1], F32)
        self.epsb = sb("epsb", [128, 1], F32)
        self.vjunk = sb("vjunk", [128, 1], F32)
        self.nbv = sb("nbv", [128, 1], F32)
        self.Rf = sb("Rf", [32, QT], F32)
        self.Rt = sb("Rt", [128, 128], F32)
        self.cw32 = sb("cw32", [128, 128], F32)
        self.cwe = sb("cwe", [128, 128], F32)
        
        self.cw_bf = sb("cw_bf", [E, QT], BF16)
        self.rm1 = sb("rm1", [128, 8], F32)
        self.rm2 = sb("rm2", [128, 8], F32)
        self.rds = sb("rds", [128, 128], F32)
        self.rden = sb("rden", [128, 8], F32)
        self.rowf = sb("rowf", [1, QT], F32)
        self.rowb = [sb(f"rowb{i}", [1, QT], BF16) for i in range(2)]
        self.drow = sb("drow", [1, QT], F32)
        self.drowb = sb("drowb", [1, QT], BF16)
        self.bcS = sb("bcS", [128, QT], F32)
        self.dbcS = sb("dbcS", [128, QT], F32)
        self.cwbS = sb("cwbS", [128, QT], F32)
        self.amem_t = sb("amem_t", [128, QT], F32)
        self.wsc = sb("wsc", [1, 4], F32)
        self.lnw = sb("lnw", [1, 2], F32)
        self.pb = [stack.enter_context(nc.psum_tensor(f"pb{i}", [128, 512], F32))
                   for i in range(8)]
        for s in ("SX", "SG", "SP", "SA", "SV", "xst0", "xst1", "wb0", "wb1",
                  "gw0", "gw1", "uw0", "uw1", "dw0", "dw1", "ost"):
            self.sems[s] = stack.enter_context(nc.semaphore(s))
            self.cnt[s] = 0

    def prep(self):
        p = self
        p.emit("vector", lambda v: v.memset(p.epsb[:, :], EPS), inc="SV")
        p.emit("vector", lambda v: v.memset(p.Rf[:, :], 0.0), inc="SV")
        p.emit("vector", lambda v: v.memset(p.cw32[:, :], 0.0), inc="SV")
        p.dma("sync", p.idb_sb[:, :], p.idb_d)
        p.dma("sync", p.idf_sb[:, :], p.idf_d)
        p.dma("sync", p.ones_sb[:, :], p.ones_d)
        cx = p.dma("sync", p.sel_sb[:, :], p.sel_d)
        p.dma("gpsimd", p.mem_sb[:, :], p.mem_d)
        p.dma("gpsimd", p.mn_sb[:, :], p.mn_d)
        p.dma("gpsimd", p.rw_sb[:, :], p.rw_d)
        v_w = p.dma("gpsimd", p.wsc[0:1, 0:1], p.mw_d)
        self.consts_x = cx
        p.fence("sync")
        p.fence("gpsimd")
        a_w = p.emit("scalar", lambda s: s.activation(p.wsc[0:1, 1:2], p.wsc[0:1, 0:1], AF.Sigmoid),
                     waits=[("SG", v_w)], inc="SA")
        v_w2 = p.emit("vector", lambda v: v.tensor_scalar(p.wsc[0:1, 2:3], p.wsc[0:1, 1:2],
                                                          -1.0, 1.0, ALU.mult, ALU.add),
                      waits=[("SA", a_w)], inc="SV")
        vwf = p.vflush()
        a_l1 = p.emit("scalar", lambda s: s.activation(p.lnw[0:1, 0:1], p.wsc[0:1, 2:3], AF.Ln),
                      waits=[("SV", vwf)], inc="SA")
        self.lnw_ready = p.emit("scalar", lambda s: s.activation(p.lnw[0:1, 1:2], p.wsc[0:1, 1:2], AF.Ln),
                                waits=[("SA", a_l1)], inc="SA")
        self.consts_g = p.cnt["SG"]
        self.xn_bf_free = None
        self.wbuf_free = [None, None]
        self.xstage_free = [None, None]
        self.outst_free = [None]
        self.out_acc_free = None
        self.xnT_free = None
        self.kT_free = None
        self.vx_free = None
        self.act_free = None
        self.ln_sc_ev = {"s2r": None, "rstdr": None, "s2br": None}
        self.scratch_ready = None
        self.ffn_first = True

    def ln_tile(self, seq, t0):
        p = self
        last = None
        for s4 in range(4):
            st = p.xstage[s4 % 2]
            row = seq * SEQ + t0 + s4 * 128
            xsem = f"xst{s4 % 2}"
            vx_ = p.dma("gpsimd", st[:, :], p.x_d[row:row + 128, :],
                        waits=[self.xstage_free[s4 % 2]], sem=xsem)
            vr = p.emit("vector", lambda v, st=st: v.reduce_sum(p.s1[:, :], st[:, :], axis=AX.X),
                        waits=[(xsem, vx_)], inc="SV")
            aq = p.emit("scalar", lambda s, st=st: s.activation(p.act[:, 0:D], st[:, :], AF.Square,
                                                                accum_out=p.s2[:, :]),
                        waits=[(xsem, vx_), self.act_free, self.ln_sc_ev["s2r"]], inc="SA")
            v1 = p.emit("vector", lambda v: v.tensor_scalar(p.mvec[:, :], p.s1[:, :],
                                                            1.0 / D, None, ALU.mult),
                        waits=[("SV", vr)], inc="SV")
            af = p.emit("scalar", lambda s: s.copy(p.s2b[:, :], p.s2[:, :]),
                        waits=[self.ln_sc_ev["s2br"]], inc="SA")
            v2 = p.emit("vector", lambda v: v.tensor_scalar(p.varv[:, :], p.s2b[:, :],
                                                            1.0 / D, None, ALU.mult),
                        waits=[("SA", af)], inc="SV")
            v3 = p.emit("vector", lambda v: v.tensor_mul(p.msq[:, :], p.mvec[:, :], p.mvec[:, :]),
                        inc="SV")
            p.vgap()
            v4 = p.emit("vector", lambda v: v.tensor_sub(p.varv[:, :], p.varv[:, :], p.msq[:, :]),
                        inc="SV")
            p.vgap()
            v4f = p.vflush()
            a1 = p.emit("scalar", lambda s: s.activation(p.rstd[:, :], p.varv[:, :], AF.Ln,
                                                         bias=p.epsb[:, :]),
                        waits=[("SV", v4f), self.ln_sc_ev["rstdr"]], inc="SA")
            a2 = p.emit("scalar", lambda s: s.activation(p.rstd[:, :], p.rstd[:, :], AF.Exp, scale=-0.5),
                        inc="SA")
            nb = p.emit("vector", lambda v: v.scalar_tensor_tensor(p.nbv[:, :], p.mvec[:, :],
                                                                   -1.0, p.rstd[:, :],
                                                                   ALU.mult, ALU.mult),
                        waits=[("SA", a2)], inc="SV")
            p.vgap()
            nbf = p.vflush()
            vn = p.emit("scalar", lambda s, st=st: s.activation(p.xn_bf[:, :], st[:, :], AF.Identity,
                                                                bias=p.nbv[:, :],
                                                                scale=p.rstd[:, :]),
                        waits=[("SV", nbf), self.xn_bf_free], inc="SA")
            tg = f"{getattr(self, 'ln_tag', '')}_s{s4}"
            self.dump_tensor(f"xst{tg}", st[:, :], [(xsem, vx_), ("SA", vn)])
            self.dump_tensor(f"s1{tg}", p.s1[:, :], [("SA", vn)])
            self.dump_tensor(f"s2{tg}", p.s2[:, :], [("SA", vn)])
            self.dump_tensor(f"s2b{tg}", p.s2b[:, :], [("SA", vn)])
            self.dump_tensor(f"mv{tg}", p.mvec[:, :], [("SA", vn)])
            self.dump_tensor(f"vv{tg}", p.varv[:, :], [("SA", vn)])
            self.dump_tensor(f"rs{tg}", p.rstd[:, :], [("SA", vn)])
            self.dump_tensor(f"nb{tg}", p.nbv[:, :], [("SA", vn)])
            self.xstage_free[s4 % 2] = ("SA", vn)
            self.ln_sc_ev["s2r"] = ("SA", af)
            self.ln_sc_ev["s2br"] = ("SV", v2)
            self.ln_sc_ev["rstdr"] = ("SA", vn)
            self.dump_tensor(f"xnb{getattr(self, 'ln_tag', '')}_s{s4}", p.xn_bf[:, :], [("SA", vn)])
            for dc in range(DC):
                bk = 6 + (dc % 2)
                vtp = p.emit("tensor",
                             lambda t, dc=dc, bk=bk: t.transpose(
                                 p.pb[bk][:, 0:64].bitcast(BF16),
                                 p.xn_bf[:, dc * 128:(dc + 1) * 128], p.idb_sb[:, :]),
                             waits=[("SA", vn)] + p.bw(bk), inc="SP")
                vev = p.emit("scalar",
                             lambda s, dc=dc, s4=s4, bk=bk: s.copy(
                                 p.xnT[:, dc * QT + s4 * 128: dc * QT + s4 * 128 + 128],
                                 p.pb[bk][:, 0:64].bitcast(BF16)),
                             waits=[("SP", vtp), self.xnT_free], inc="SA")
                p.bank_free[bk] = ("SA", vev)
                last = vev
            self.xn_bf_free = ("SA", last)
        return ("SA", last)

    def pass_a_ktile(self, seq, kt, tev):
        p = self
        for mat, wt_d, which in (("k", p.wkt_d, 0), ("v", p.wvt_d, 1)):
            for jc in range(JC):
                slot = (jc + which * JC) % 2
                wb = p.wbuf[slot]
                vg = p.dma("gpsimd", wb[:, :], wt_d[jc * 128:(jc + 1) * 128, :],
                           waits=[self.wbuf_free[slot]], sem=f"wb{slot}")
                bk = 0 if which == 0 else 2
                bk += jc % 2
                m = None
                for dc in range(DC):
                    m = p.emit("tensor",
                               lambda t, wb=wb, dc=dc, bk=bk, first=(dc == 0), lastc=(dc == DC - 1):
                               t.matmul(p.pb[bk][:, :], wb[:, dc * 128:(dc + 1) * 128],
                                        p.xnT[:, dc * QT:dc * QT + QT],
                                        start=first, stop=lastc),
                               waits=([(f"wb{slot}", vg), tev] + p.bw(bk)) if dc == 0 else [],
                               inc="SP" if dc == DC - 1 else None)
                self.wbuf_free[slot] = ("SP", m)
                if which == 0:
                    col = jc * SEQ + kt * QT
                    ae = p.emit("scalar",
                                lambda s, col=col, bk=bk: s.copy(
                                    p.kT[:, col:col + QT], p.pb[bk][:, :]),
                                waits=[("SP", m), self.kT_free], inc="SA")
                    p.bank_free[bk] = ("SA", ae)
                else:
                    ae = p.emit("scalar",
                                lambda s, bk=bk: s.copy(p.vtmp[:, :], p.pb[bk][:, :]),
                                waits=[("SP", m)], inc="SA")
                    p.bank_free[bk] = ("SA", ae)
                    for s4 in range(4):
                        tc = kt * 4 + s4
                        tb = 4 + (s4 % 2)
                        vtp = p.emit("tensor",
                                     lambda t, s4=s4, tb=tb: t.transpose(
                                         p.pb[tb][:, 0:64].bitcast(BF16),
                                         p.vtmp[:, s4 * 128:(s4 + 1) * 128], p.idb_sb[:, :]),
                                     waits=[("SA", ae)] + p.bw(tb), inc="SP")
                        off = tc * 520 + 2 * jc * 65
                        vev = p.emit("scalar",
                                     lambda s, off=off, tb=tb: s.copy(
                                         p.vx[:, off:off + 130]
                                         .rearrange("p (h d) -> p h d", d=65)[:, :, 0:64],
                                         p.pb[tb][:, 0:64].bitcast(BF16)
                                         .rearrange("p (h d) -> p h d", d=64)),
                                     waits=[("SP", vtp), self.vx_free], inc="SA")
                        p.bank_free[tb] = ("SA", vev)
                        self.pass_a_last = vev

    def router(self, tev):
        p = self
        bk = 6
        m = None
        for dc in range(DC):
            m = p.emit("tensor",
                       lambda t, dc=dc: t.matmul(p.pb[bk][0:8, :],
                                                 p.rw_sb[:, dc * E:(dc + 1) * E],
                                                 p.xnT[:, dc * QT:dc * QT + QT],
                                                 start=(dc == 0), stop=(dc == DC - 1)),
                       waits=([tev, ("SG", self.consts_g)] + p.bw(bk)) if dc == 0 else [],
                       inc="SP" if dc == DC - 1 else None)
        ve = p.emit("vector", lambda v: v.tensor_copy(p.Rf[0:8, :], p.pb[bk][0:8, :]),
                    waits=[("SP", m), self.rf_free], inc="SV")
        p.bank_free[bk] = ("SV", ve)
        vef = p.vflush()
        vtp = None
        for b in range(4):
            vtp = p.emit("tensor", lambda t, b=b: t.transpose(
                p.pb[7][:, b * 8:(b + 1) * 8], p.Rf[0:8, b * 128:(b + 1) * 128],
                p.idf_sb[0:8, 0:8]),
                waits=([("SV", vef)] if b == 0 else []) + (p.bw(7) if b == 0 else []),
                inc="SP" if b == 3 else None)
        vrt = p.emit("vector", lambda v: v.tensor_copy(p.Rt[0:128, 0:32], p.pb[7][:, 0:32]),
                     waits=[("SP", vtp), self.rt_free], inc="SV")
        p.bank_free[7] = ("SV", vrt)
        p.vgap()
        self.dump_tensor("Rf", p.Rf[:, :], [("SV", ve)])
        self.dump_tensor("Rt", p.Rt[:, :], [("SV", vrt)])
        R3 = p.Rt[0:128, 0:32].rearrange("p (b e) -> p b e", e=E)
        MSK1 = p.rds[:, 0:32].rearrange("p (b e) -> p b e", e=E)
        S3p = p.rds[:, 32:64].rearrange("p (b e) -> p b e", e=E)
        MSK2 = p.rds[:, 64:96].rearrange("p (b e) -> p b e", e=E)
        MSKD = p.rds[:, 96:128].rearrange("p (b e) -> p b e", e=E)
        vm1 = p.emit("vector", lambda v, R3=R3: v.reduce_max(
            p.rm1[:, 0:4].rearrange("p (b o) -> p b o", o=1), R3, axis=AX.X), inc="SV")
        m1b = p.rm1[:, 0:4].rearrange("p (b o) -> p b o", o=1).broadcast_to([128, 4, E])
        p.vgap()
        C3 = p.cw32[0:128, 0:32].rearrange("p (b e) -> p b e", e=E)
        vsub = p.emit("vector", lambda v, R3=R3, C3=C3, m1b=m1b: v.tensor_tensor(
            C3, R3, m1b, ALU.subtract), waits=[self.cw32_free], inc="SV")
        p.vgap()
        vms = p.emit("vector", lambda v, C3=C3, MSK1=MSK1: v.tensor_scalar(
            MSK1, C3, 0.0, None, ALU.is_ge), inc="SV")
        p.vgap()
        vsp = p.emit("vector", lambda v, R3=R3, MSK1=MSK1, S3p=S3p: v.scalar_tensor_tensor(
            S3p, MSK1, -1e30, R3, ALU.mult, ALU.add), inc="SV")
        p.vgap()
        vm2 = p.emit("vector", lambda v, S3p=S3p: v.reduce_max(
            p.rm2[:, 0:4].rearrange("p (b o) -> p b o", o=1), S3p, axis=AX.X), inc="SV")
        p.vgap()
        vd = p.emit("vector", lambda v: v.tensor_sub(p.rm2[:, 4:8], p.rm2[:, 0:4],
                                                     p.rm1[:, 0:4]), inc="SV")
        p.vgap()
        vdf = p.vflush()
        ae2 = p.emit("scalar", lambda s: s.activation(p.rden[:, 0:4], p.rm2[:, 4:8], AF.Exp),
                     waits=[("SV", vdf)], inc="SA")
        vd1 = p.emit("vector", lambda v: v.tensor_scalar(p.rden[:, 4:8], p.rden[:, 0:4],
                                                         1.0, None, ALU.add),
                     waits=[("SA", ae2)], inc="SV")
        p.vgap()
        vd2 = p.emit("vector", lambda v: v.reciprocal(p.rden[:, 0:4], p.rden[:, 4:8]), inc="SV")
        p.vgap()
        self.dump_tensor("rm1", p.rm1[:, :], [("SV", vm1)])
        self.dump_tensor("rm2", p.rm2[:, :], [("SV", vd)])
        self.dump_tensor("rdsA", p.rds[:, :], [("SV", vm2)])
        self.dump_tensor("rden", p.rden[:, :], [("SV", vd2)])
        m2b = p.rm2[:, 0:4].rearrange("p (b o) -> p b o", o=1).broadcast_to([128, 4, E])
        vd2b = p.emit("vector", lambda v, R3=R3, m2b=m2b, S3p=S3p: v.tensor_tensor(
            S3p, R3, m2b, ALU.subtract), inc="SV")
        p.vgap()
        vmk = p.emit("vector", lambda v, S3p=S3p, MSK2=MSK2: v.tensor_scalar(
            MSK2, S3p, 0.0, None, ALU.is_ge), inc="SV")
        p.vgap()
        denb = p.rden[:, 0:4].rearrange("p (b o) -> p b o", o=1).broadcast_to([128, 4, E])
        vmk2 = p.emit("vector", lambda v, MSK2=MSK2, denb=denb, MSKD=MSKD: v.tensor_tensor(
            MSKD, MSK2, denb, ALU.mult), inc="SV")
        p.vgap()
        E3 = p.cwe[0:128, 0:32].rearrange("p (b e) -> p b e", e=E)
        vsf = p.vflush()
        ae1 = p.emit("scalar", lambda s: s.activation(p.cwe[0:128, 0:32],
                                                      p.cw32[0:128, 0:32], AF.Exp),
                     waits=[("SV", vsf), self.cwe_free], inc="SA")
        vc1 = p.emit("vector", lambda v, C3=C3, E3=E3, MSKD=MSKD: v.tensor_tensor(
            C3, E3, MSKD, ALU.mult), waits=[("SA", ae1), ("SV", vmk2)], inc="SV")
        self.dump_tensor("rdsB", p.rds[:, :], [("SV", vmk2)])
        self.dump_tensor("cw32", p.cw32[:, :], [("SV", vc1)])
        vc2f = p.vflush()
        vcb = None
        for b in range(4):
            tb = 6 + (b % 2)
            mtb = p.emit("tensor", lambda t, b=b, tb=tb: t.transpose(
                p.pb[tb][0:8, 0:128], p.cw32[0:128, b * 8:(b + 1) * 8],
                p.idf_sb[:, :]),
                waits=[("SV", vc2f)] + p.bw(tb), inc="SP")
            vcb = p.emit("scalar", lambda s, b=b, tb=tb: s.copy(
                p.cw_bf[:, b * 128:(b + 1) * 128], p.pb[tb][0:E, 0:128]),
                waits=[("SP", mtb)], inc="SA")
            p.bank_free[tb] = ("SA", vcb)
        self.cw32_free = ("SV", vc2f)
        self.cwe_free = ("SV", vc1)
        self.rf_free = ("SV", vef)
        self.rt_free = ("SV", p.cnt["SV"])
        self.dump_tensor("cw", p.cw_bf[:, :], [("SA", vcb)])
        return ("SA", vcb)

    def qgroup(self, g, seq, tev):
        p = self
        ic_evs = []
        for icl in range(4):
            ic = g * 4 + icl
            slot = ic % 2
            wb = p.wbuf[slot]
            vg = p.dma("gpsimd", wb[:, :], p.wqt_d[ic * 128:(ic + 1) * 128, :],
                       waits=[self.wbuf_free[slot]], sem=f"wb{slot}")
            bk = 6 + (icl % 2)
            m = None
            for dc in range(DC):
                m = p.emit("tensor",
                           lambda t, wb=wb, dc=dc, bk=bk: t.matmul(
                               p.pb[bk][:, :], wb[:, dc * 128:(dc + 1) * 128],
                               p.xnT[:, dc * QT:dc * QT + QT],
                               start=(dc == 0), stop=(dc == DC - 1)),
                           waits=([(f"wb{slot}", vg), tev] + p.bw(bk)) if dc == 0 else [],
                           inc="SP" if dc == DC - 1 else None)
            self.wbuf_free[slot] = ("SP", m)
            ae = p.emit("scalar", lambda s, icl=icl, bk=bk: s.copy(
                p.qT[:, icl * QT:(icl + 1) * QT], p.pb[bk][:, :]),
                waits=[("SP", m), self.qT_free], inc="SA")
            if self.stop_after == "qmm":
                p.bank_free[bk] = ("SA", ae)
                ic_evs.append(("SA", ae))
                continue
            ELU_STOP = self.stop_after if (self.stop_after or "").startswith("qe") else None
            ve1 = p.emit("vector", lambda v, bk=bk: v.tensor_scalar(
                p.elu0[:, :], p.pb[bk][:, :], 0.0, None, ALU.min),
                waits=[("SP", m), ("SA", ae), self.elu_free], inc="SV")
            if ELU_STOP == "qe1":
                self.elu_free = ("SV", ve1)
                p.bank_free[bk] = ("SV", ve1)
                ic_evs.append(("SA", ae))
                continue
            ae2 = p.emit("scalar", lambda s: s.activation(p.elu1[:, :], p.elu0[:, :], AF.Exp),
                         waits=[("SV", ve1)], inc="SA")
            if ELU_STOP == "qe2":
                self.elu_free = ("SA", ae2)
                p.bank_free[bk] = ("SV", ve1)
                ic_evs.append(("SA", ae))
                continue
            ve2 = p.emit("vector", lambda v, bk=bk: v.tensor_scalar(
                p.elu0[:, :], p.pb[bk][:, :], 0.0, -1.0, ALU.max, ALU.add),
                waits=[("SA", ae2)], inc="SV")
            if ELU_STOP == "qe3":
                self.elu_free = ("SV", ve2)
                p.bank_free[bk] = ("SV", ve2)
                ic_evs.append(("SA", ae))
                continue
            p.vgap()
            ve3 = p.emit("vector", lambda v, icl=icl: v.tensor_add(
                p.sigma[:, icl * QT:(icl + 1) * QT], p.elu0[:, :], p.elu1[:, :]),
                waits=[self.sigma_free], inc="SV")
            self.elu_free = ("SV", ve3)
            p.bank_free[bk] = ("SV", ve2)
            ic_evs.append(("SA", ae))
        self.elu_free = ("SV", p.vflush())
        q_ready = ic_evs
        self.dump_tensor("qT", p.qT[:, :], q_ready)
        self.dump_tensor("sigma", p.sigma[:, :], [self.elu_free])
        if self.stop_after in ("qproj", "qmm", "qe1", "qe2", "qe3"):
            self.attn_last = 1
            return

        npair = 1 if self.stop_after == "pair0" else 4
        for j in range(npair):
            h0 = 2 * j
            idx0 = g * 8 + h0
            fcol = (idx0 // 2)
            exp_ev = [None] * 32
            ad_m = [None, None]
            for tc in range(16):
                sb0 = (tc % 2) * 2
                sb1 = sb0 + 1
                m0 = p.emit("tensor",
                            lambda t, j=j, tc=tc, sb0=sb0: t.matmul(
                                p.pb[sb0][:, :],
                                p.kT[0:64, j * SEQ + tc * 128: j * SEQ + tc * 128 + 128],
                                p.qT[0:64, j * QT:(j + 1) * QT],
                                start=True, stop=True, tile_position=(0, 0)),
                            waits=[q_ready[j], self.kT_ready] + p.bw(sb0), inc=None)
                m1 = p.emit("tensor",
                            lambda t, j=j, tc=tc, sb1=sb1: t.matmul(
                                p.pb[sb1][:, :],
                                p.kT[64:128, j * SEQ + tc * 128: j * SEQ + tc * 128 + 128],
                                p.qT[64:128, j * QT:(j + 1) * QT],
                                start=True, stop=True, tile_position=(64, 0)),
                            waits=p.bw(sb1), inc="SP")
                for hh in range(2):
                    pt = p.ptil[(2 * tc + hh) % 4]
                    sbk = sb0 + hh
                    aev = p.emit("scalar",
                                 lambda s, pt=pt, sbk=sbk: s.activation(
                                     pt[:, :], p.pb[sbk][:, :], AF.Exp, scale=SCALE),
                                 waits=[("SP", m1)], inc="SA")
                    exp_ev[2 * tc + hh] = aev
                    p.bank_free[sbk] = ("SA", aev)
                for hh in range(2):
                    h = h0 + hh
                    pt = p.ptil[(2 * tc + hh) % 4]
                    ab = 4 + hh
                    off = tc * 520 + h * 65
                    ad_m[hh] = p.emit("tensor",
                                      lambda t, off=off, pt=pt, ab=ab, tc=tc: t.matmul(
                                          p.pb[ab][0:65, :], p.vx[:, off:off + 65], pt[:, :],
                                          start=(tc == 0), stop=(tc == 15)),
                                      waits=[("SA", exp_ev[2 * tc + hh]), self.vx_ready]
                                      + (p.bw(ab) if tc == 0 else []),
                                      inc="SP" if tc == 15 else None)
            for hh in range(2):
                ab = 4 + hh
                bb = 6 + hh
                al = p.emit("scalar", lambda s, ab=ab: s.activation(
                    p.rowf[0:1, :], p.pb[ab][64:65, :], AF.Ln),
                    waits=[("SP", ad_m[hh]), self.rowf_free], inc="SA")
                ae = p.emit("scalar", lambda s, hh=hh: s.activation(
                    p.rowb[hh][0:1, :], p.rowf[0:1, :], AF.Exp, scale=-1.0,
                    bias=p.lnw[0:1, 0:1]),
                    waits=[("SA", self.lnw_ready), self.rowb_free[hh]], inc="SA")
                self.rowf_free = ("SA", ae)
                mb = p.emit("tensor", lambda t, hh=hh, bb=bb: t.matmul(
                    p.pb[bb][0:64, :], p.ones_sb[0:1, 0:64], p.rowb[hh][0:1, :],
                    start=True, stop=True),
                    waits=[("SA", ae)] + p.bw(bb), inc="SP")
                be = p.emit("scalar", lambda s, hh=hh, bb=bb: s.copy(
                    p.bcS[hh * 64:hh * 64 + 64, :], p.pb[bb][0:64, :]),
                    waits=[("SP", mb), self.bcS_free], inc="SA")
                p.bank_free[bb] = ("SA", be)
                self.rowb_free[hh] = ("SP", mb)
                vo = p.emit("vector", lambda v, hh=hh, ab=ab, fcol=fcol: v.tensor_tensor(
                    p.out_acc[hh * 64:hh * 64 + 64, fcol * QT:(fcol + 1) * QT],
                    p.pb[ab][0:64, :], p.bcS[hh * 64:hh * 64 + 64, :], ALU.mult),
                    waits=[("SA", be), self.out_acc_free], inc="SV")
                p.bank_free[ab] = ("SV", vo)
                self.bcS_free = ("SV", vo)
                self.attn_last = vo

        if self.stop_after in ("pair0", "heads"):
            return
        dm = None
        for icl in range(4):
            dm = p.emit("tensor", lambda t, icl=icl: t.matmul(
                p.pb[4][0:1, :], p.mn_sb[:, icl:icl + 1],
                p.sigma[:, icl * QT:(icl + 1) * QT],
                start=(icl == 0), stop=(icl == 3)),
                waits=([self.elu_free, ("SG", self.consts_g)] + p.bw(4)) if icl == 0 else [],
                inc="SP" if icl == 3 else None)
        vdr = p.emit("vector", lambda v: v.reciprocal(p.drow[0:1, :], p.pb[4][0:1, :]),
                     waits=[("SP", dm), self.drow_free], inc="SV")
        vdb = p.emit("vector", lambda v: v.tensor_scalar(p.drowb[0:1, :], p.drow[0:1, :],
                                                         p.wsc[0:1, 1:2], None, ALU.mult),
                     waits=[self.drowb_free], inc="SV")
        p.bank_free[4] = ("SV", vdr)
        self.drow_free = ("SV", vdb)
        vdb = p.vflush()
        mdb = p.emit("tensor", lambda t: t.matmul(
            p.pb[5][:, :], p.ones_sb[0:1, :], p.drowb[0:1, :], start=True, stop=True),
            waits=[("SV", vdb)] + p.bw(5), inc="SP")
        bde = p.emit("scalar", lambda s: s.copy(p.dbcS[:, :], p.pb[5][:, :]),
                     waits=[("SP", mdb), self.dbcS_free], inc="SA")
        p.bank_free[5] = ("SA", bde)
        self.drowb_free = ("SP", mdb)
        vt_ = None
        for jcl in range(4):
            ab = jcl % 2
            mm = None
            for icl in range(4):
                mm = p.emit("tensor", lambda t, icl=icl, jcl=jcl, ab=ab: t.matmul(
                    p.pb[ab][:, :], p.mem_sb[:, icl * DKV + jcl * 128: icl * DKV + jcl * 128 + 128],
                    p.sigma[:, icl * QT:(icl + 1) * QT],
                    start=(icl == 0), stop=(icl == 3)),
                    waits=(p.bw(ab) + [self.elu_free, ("SG", self.consts_g)])
                    if icl == 0 else [], inc="SP" if icl == 3 else None)
            vt_ = p.emit("vector", lambda v, ab=ab: v.tensor_tensor(
                p.amem_t[:, :], p.pb[ab][:, :], p.dbcS[:, :], ALU.mult),
                waits=[("SP", mm), ("SA", bde), self.amem_free], inc="SV")
            p.vgap()
            fcol = g * 4 + jcl
            va = p.emit("vector", lambda v, fcol=fcol: v.tensor_add(
                p.out_acc[:, fcol * QT:(fcol + 1) * QT],
                p.out_acc[:, fcol * QT:(fcol + 1) * QT], p.amem_t[:, :]),
                inc="SV")
            p.bank_free[ab] = ("SV", vt_)
            self.amem_free = ("SV", va)
            self.attn_last = va
        self.dbcS_free = ("SV", vt_)
        self.qT_free = ("SP", p.cnt["SP"])
        self.sigma_free = ("SP", p.cnt["SP"])

    def ffn_expert(self, e, qt_idx, tev, cw_ev):
        p = self
        mcw = p.emit("tensor", lambda t, e=e: t.matmul(
            p.pb[6][:, :], p.sel_sb[:, e * 128:(e + 1) * 128], p.cw_bf[:, :],
            start=True, stop=True),
            waits=[cw_ev, ("SX", self.consts_x)] + p.bw(6), inc="SP")
        cbe = p.emit("scalar", lambda s: s.copy(p.cwbS[:, :], p.pb[6][:, :]),
                     waits=[("SP", mcw), self.cwbS_free], inc="SA")
        p.bank_free[6] = ("SA", cbe)

        act_evs = []
        for fc in range(FC):
            slot = fc % 2
            r0 = (e * FC + fc) * 128
            if self.ffn_first:
                gsrc, usrc = self.gate_d[r0:r0 + 128, :], self.up_d[r0:r0 + 128, :]
            else:
                gsrc, usrc = self.gsc_d[r0:r0 + 128, :], self.usc_d[r0:r0 + 128, :]
            sready = [] if self.ffn_first else [self.scratch_ready]
            vgg = p.dma("gpsimd", p.gw[slot][:, :], gsrc,
                        waits=[self.gw_free[slot]] + sready, sem=f"gw{slot}")
            vgu = p.dma("gpsimd", p.uw[slot][:, :], usrc,
                        waits=[self.uw_free[slot]] + sready, sem=f"uw{slot}")
            gst = ust = None
            if self.ffn_first:
                gst = p.dma("sync", self.gsc_d[r0:r0 + 128, :], p.gw[slot][:, :],
                            waits=[(f"gw{slot}", vgg)])
                ust = p.dma("sync", self.usc_d[r0:r0 + 128, :], p.uw[slot][:, :],
                            waits=[(f"uw{slot}", vgu)])
            gb = (fc % 2) * 2
            ub = gb + 1
            mg = None
            for dc in range(DC):
                mg = p.emit("tensor", lambda t, slot=slot, dc=dc, gb=gb: t.matmul(
                    p.pb[gb][:, :], p.gw[slot][:, dc * 128:(dc + 1) * 128],
                    p.xnT[:, dc * QT:dc * QT + QT],
                    start=(dc == 0), stop=(dc == DC - 1)),
                    waits=([(f"gw{slot}", vgg), tev] + p.bw(gb)) if dc == 0 else [],
                    inc="SP" if dc == DC - 1 else None)
            mu = None
            for dc in range(DC):
                mu = p.emit("tensor", lambda t, slot=slot, dc=dc, ub=ub: t.matmul(
                    p.pb[ub][:, :], p.uw[slot][:, dc * 128:(dc + 1) * 128],
                    p.xnT[:, dc * QT:dc * QT + QT],
                    start=(dc == 0), stop=(dc == DC - 1)),
                    waits=([(f"uw{slot}", vgu)] + p.bw(ub)) if dc == 0 else [],
                    inc="SP" if dc == DC - 1 else None)
            self.gw_free[slot] = ("SP", mg) if gst is None else ("SX", gst)
            self.uw_free[slot] = ("SP", mu) if ust is None else ("SX", ust)
            if gst is not None:
                self.gw_free[slot] = [("SP", mg), ("SX", gst)]
                self.uw_free[slot] = [("SP", mu), ("SX", ust)]
            sil = p.emit("scalar", lambda s, gb=gb: s.activation(
                p.sgb[:, :], p.pb[gb][:, :], AF.Silu),
                waits=[("SP", mg), ("SA", cbe), self.sgb_free], inc="SA")
            p.bank_free[gb] = ("SA", sil)
            vu = p.emit("vector", lambda v, ub=ub: v.tensor_tensor(
                p.ubb[:, :], p.pb[ub][:, :], p.cwbS[:, :], ALU.mult),
                waits=[("SP", mu), ("SA", cbe), self.ubb_free], inc="SV")
            p.bank_free[ub] = ("SV", vu)
            va = p.emit("vector", lambda v, fc=fc: v.tensor_tensor(
                p.act[:, fc * QT:(fc + 1) * QT], p.sgb[:, :], p.ubb[:, :], ALU.mult),
                waits=[("SA", sil), self.act_free], inc="SV")
            self.sgb_free = ("SV", va)
            self.ubb_free = ("SV", va)
            act_evs.append(("SV", va))
        act_all = ("SV", p.vflush())

        for dq in range(8):
            slot = dq % 2
            r0 = (e * 8 + dq) * 128
            dsrc = (self.down_d if self.ffn_first else self.dsc_d)[r0:r0 + 128, :]
            vgd = p.dma("gpsimd", p.dwb[slot][:, :], dsrc,
                        waits=[self.dw_free[slot]] + ([] if self.ffn_first else [self.scratch_ready]),
                        sem=f"dw{slot}")
            dst_ = None
            if self.ffn_first:
                dst_ = p.dma("sync", self.dsc_d[r0:r0 + 128, :], p.dwb[slot][:, :],
                             waits=[(f"dw{slot}", vgd)])
            for dcl in range(2):
                dc = dq * 2 + dcl
                db = 4 + (dcl % 2)
                md = None
                for fc in range(FC):
                    md = p.emit("tensor", lambda t, slot=slot, fc=fc, dcl=dcl, db=db: t.matmul(
                        p.pb[db][:, :], p.dwb[slot][:, fc * 256 + dcl * 128: fc * 256 + dcl * 128 + 128],
                        p.act[:, fc * QT:(fc + 1) * QT],
                        start=(fc == 0), stop=(fc == FC - 1)),
                        waits=([(f"dw{slot}", vgd), act_all] + p.bw(db)) if fc == 0 else [],
                        inc="SP" if fc == FC - 1 else None)
                if dcl == 1:
                    self.dw_free[slot] = ("SP", md) if dst_ is None else [("SP", md), ("SX", dst_)]
                vo = p.emit("vector", lambda v, dc=dc, db=db: v.tensor_add(
                    p.out_acc[:, dc * QT:(dc + 1) * QT],
                    p.out_acc[:, dc * QT:(dc + 1) * QT], p.pb[db][:, :]),
                    waits=[("SP", md)], inc="SV")
                p.bank_free[db] = ("SV", vo)
                self.ffn_last = vo
        self.ffn_flush = p.vflush()
        self.act_free = ("SP", p.cnt["SP"])
        self.cwbS_free = ("SV", self.ffn_last)

    def epilogue(self, seq, qt):
        p = self
        for t4 in range(4):
            st = p.xstage[t4 % 2]
            row = seq * SEQ + qt * QT + t4 * 128
            xsem = f"xst{t4 % 2}"
            vx_ = p.dma("gpsimd", st[:, :], p.x_d[row:row + 128, :],
                        waits=[self.xstage_free[t4 % 2]], sem=xsem)
            ob = p.outst[0]
            last_add = None
            for fc in range(DC):
                tb = 6 + (fc % 2)
                vtp = p.emit("tensor", lambda t, fc=fc, t4=t4, tb=tb: t.transpose(
                    p.pb[tb][:, 0:128], p.out_acc[:, fc * QT + t4 * 128: fc * QT + t4 * 128 + 128],
                    p.idf_sb[:, :]),
                    waits=([("SV", self.ffn_flush)] if fc == 0 else []) + p.bw(tb), inc="SP")
                va = p.emit("vector", lambda v, fc=fc, tb=tb, st=st, ob=ob: v.tensor_tensor(
                    ob[:, fc * 128:(fc + 1) * 128], st[:, fc * 128:(fc + 1) * 128],
                    p.pb[tb][:, 0:128], ALU.add),
                    waits=[("SP", vtp), (xsem, vx_), self.outst_free[0]], inc="SV")
                p.bank_free[tb] = ("SV", va)
                last_add = va
            laf = p.vflush()
            sx = p.dma("sync", p.out_d[row:row + 128, :], ob[:, :],
                       waits=[("SV", laf)], sem="ost")
            self.outst_free[0] = ("ost", sx)
            self.xstage_free[t4 % 2] = ("SV", last_add)
        self.out_acc_free = ("SP", p.cnt["SP"])

    def build(self):
        p = self
        self.declare()
        with ExitStack() as stack:
            self.alloc(stack)
            self.prep()
            self.elu_free = None
            self.qT_free = None
            self.sigma_free = None
            self.rowf_free = None
            self.rowb_free = [None, None]
            self.bcS_free = None
            self.drow_free = None
            self.drowb_free = None
            self.dbcS_free = None
            self.amem_free = None
            self.cwbS_free = None
            self.sgb_free = None
            self.ubb_free = None
            self.gw_free = [None, None]
            self.uw_free = [None, None]
            self.dw_free = [None, None]
            self.rf_free = None
            self.cw32_free = None
            self.rt_free = None
            self.cwe_free = None
            self.pass_a_sem = "SV"
            self.ffn_last = None
            self.attn_last = None
            self.consts_g = self.cnt["SG"]
            SA_ = self.stop_after
            for seq in range(self.n_seq):
                if SA_ == "prep":
                    break
                vo = p.emit("vector", lambda v: v.memset(
                    p.vx[:, :].rearrange("p (t h d) -> p t h d", h=8, d=65)[:, :, :, 64:65], 1.0),
                    waits=[self.vx_free], inc="SV")
                self.vx_free = ("SV", vo)
                for kt in range(NT):
                    if SA_ == "passa0" and kt > 0:
                        break
                    self.ln_tag = f"A{kt}"
                    tev = self.ln_tile(seq, kt * QT)
                    if SA_ in ("ln0", "ln0s"):
                        break
                    self.pass_a_ktile(seq, kt, tev)
                    self.xnT_free = ("SP", self.cnt["SP"])
                self.kT_ready = ("SA", self.cnt["SA"])
                self.vx_ready = ("SA", self.pass_a_last)
                self.dump_tensor("kT", p.kT[:, :], [self.kT_ready])
                self.dump_tensor("vx", p.vx[:, :], [self.vx_ready])
                if SA_ in ("prep", "ln0", "passa0", "passa"):
                    break
                for qt in range(self.n_tiles):
                    self.ln_tag = f"B{qt}"
                    tev = self.ln_tile(seq, qt * QT)
                    self.dump_tensor("xnT", p.xnT[:, :], [tev])
                    if SA_ == "lnq":
                        break
                    cw_ev = self.router(tev)
                    if SA_ == "router":
                        break
                    if SA_ in ("qmm", "qproj", "pair0", "heads", "qe1", "qe2", "qe3"):
                        self.qgroup(0, seq, tev)
                        break
                    ngq = NQ if SA_ is None or not SA_.startswith("qg") else int(SA_[2:]) + 1
                    for g in range(ngq):
                        self.qgroup(g, seq, tev)
                    if SA_ is not None and (SA_.startswith("qg") or SA_ in ("qmm", "qproj", "pair0", "heads", "qe1", "qe2", "qe3")):
                        break
                    self.dump_tensor("outacc_attn", p.out_acc[:, :], [("SV", self.attn_last)])
                    self.dump_tensor("cw2", p.cw_bf[:, :], [("SV", self.attn_last)])
                    nfe = E if SA_ is None or not SA_.startswith("ffn") else int(SA_[3:]) + 1
                    self.ffn_first = (seq == 0 and qt == 0)
                    for e in range(nfe):
                        self.ffn_expert(e, qt, tev, cw_ev)
                    if self.ffn_first:
                        self.fence("sync")
                        self.scratch_ready = ("SX", self.cnt["SX"])
                        self.ffn_first = False
                    if SA_ is not None and SA_.startswith("ffn"):
                        break
                    self.dump_tensor("outacc", p.out_acc[:, :], [("SV", self.ffn_last)])
                    self.epilogue(seq, qt)
                    self.xnT_free = ("SP", self.cnt["SP"])
                if SA_ is not None:
                    break
                self.kT_free = ("SP", self.cnt["SP"])
                self.vx_free = ("SP", self.cnt["SP"])

            final_o = self.cnt["ost"]
            final_x = self.cnt["SX"]
            p.emit("sync", lambda s: s.wait_ge(self.sems["ost"], final_o),
                   waits=[("SX", final_x)])

            block = stack.enter_context(self.nc.Block())

            def runner(eng_name):
                def _run(eng):
                    for fn, waits, inc, n in self.ops[eng_name]:
                        for s, v in waits:
                            eng.wait_ge(self.sems[s], v)
                        inst = fn(eng)
                        if inc is not None:
                            inst.then_inc(self.sems[inc], n)
                return _run

            block.tensor(runner("tensor"))
            block.vector(runner("vector"))
            block.scalar(runner("scalar"))
            block.sync(runner("sync"))
            block.gpsimd(runner("gpsimd"))
        return self.nc

    vx_free = None
    kT_free = None
    kT_ready = None
    vx_ready = None
    pass_a_last = None


def build_program(n_seq=NSEQ, n_tiles=NT, dump=(), stop_after=None):
    b = Builder(n_seq=n_seq, n_tiles=n_tiles, dump=dump, stop_after=stop_after)
    nc = b.build()
    return nc, b


_CONSTS = None


def _consts():
    global _CONSTS
    if _CONSTS is None:
        identbf = np.eye(128, dtype=ml_dtypes.bfloat16)
        identf = np.eye(128, dtype=np.float32)
        onesbf = np.ones((1, 128), dtype=ml_dtypes.bfloat16)
        selbf = np.zeros((E, E * 128), dtype=ml_dtypes.bfloat16)
        for e in range(E):
            selbf[e, e * 128:(e + 1) * 128] = 1.0
        _CONSTS = dict(identbf=identbf, identf=identf, onesbf=onesbf, selbf=selbf)
    return _CONSTS


def _relayout(wq, wk, wv, mem, mn, rw, gate, up, down):
    wqt = np.ascontiguousarray(wq.reshape(16, 128, 16, 128).transpose(0, 3, 2, 1)
                               .reshape(DC * 128, D))
    wkt = np.ascontiguousarray(wk.reshape(4, 128, 16, 128).transpose(0, 3, 2, 1)
                               .reshape(JC * 128, D))
    wvt = np.ascontiguousarray(wv.reshape(4, 128, 16, 128).transpose(0, 3, 2, 1)
                               .reshape(JC * 128, D))
    mem_r = np.ascontiguousarray(mem.reshape(4, 128, DKV).transpose(1, 0, 2)
                                 .reshape(128, JC * DKV))
    mn_r = np.ascontiguousarray(mn.reshape(4, 128).T)
    rw_r = np.ascontiguousarray(rw.reshape(16, 128, E).transpose(1, 0, 2)
                                .reshape(128, DC * E))
    gate_r = np.ascontiguousarray(gate.reshape(E, 16, 128, 8, 128)
                                  .transpose(0, 3, 2, 1, 4).reshape(E * FC * 128, 2048))
    up_r = np.ascontiguousarray(up.reshape(E, 16, 128, 8, 128)
                                .transpose(0, 3, 2, 1, 4).reshape(E * FC * 128, 2048))
    down_r = np.ascontiguousarray(down.reshape(E, 8, 128, 8, 256)
                                  .transpose(0, 3, 2, 1, 4).reshape(E * 8 * 128, 2048))
    return wqt, wkt, wvt, mem_r, mn_r, rw_r, gate_r, up_r, down_r


def kernel(L0_attn_x, L0_attn_Wq, L0_attn_Wk, L0_attn_Wv, L0_attn_memory,
           L0_attn_memory_norm, L0_attn_memory_weight, L0_ffn_router_w,
           L0_ffn_gate_proj, L0_ffn_up_proj, L0_ffn_down_proj):
    x = np.ascontiguousarray(np.asarray(L0_attn_x, dtype=np.float32))
    wq = np.asarray(L0_attn_Wq, dtype=np.float32)
    wk = np.asarray(L0_attn_Wk, dtype=np.float32)
    wv = np.asarray(L0_attn_Wv, dtype=np.float32)
    mem = np.asarray(L0_attn_memory, dtype=np.float32)
    mn = np.asarray(L0_attn_memory_norm, dtype=np.float32)
    mw = np.asarray(L0_attn_memory_weight, dtype=np.float32).reshape(1, 1)
    rw = np.asarray(L0_ffn_router_w, dtype=np.float32)
    gate = np.asarray(L0_ffn_gate_proj, dtype=np.float32)
    up = np.asarray(L0_ffn_up_proj, dtype=np.float32)
    down = np.asarray(L0_ffn_down_proj, dtype=np.float32)
    wqt, wkt, wvt, mem_r, mn_r, rw_r, gate_r, up_r, down_r = _relayout(
        wq, wk, wv, mem, mn, rw, gate, up, down)
    c = _consts()

    nc, _ = build_program()
    in_maps = []
    for core in range(8):
        xs = x[2 * core:2 * core + 2].reshape(NSEQ * SEQ, D)
        in_maps.append(dict(x=np.ascontiguousarray(xs), wqt=wqt, wkt=wkt, wvt=wvt,
                            mem=mem_r, memnorm=mn_r, memw=mw, routw=rw_r,
                            gate=gate_r, up=up_r, down=down_r,
                            identbf=c["identbf"], identf=c["identf"],
                            onesbf=c["onesbf"], selbf=c["selbf"]))
    res = run_bass_kernel_spmd(nc, in_maps, core_ids=list(range(8)))
    outs = [res.results[i]["out"].reshape(NSEQ, SEQ, D) for i in range(8)]
    return np.concatenate(outs, axis=0)
